# version 03
# baseline (speedup 1.0000x reference)
import contextlib
import math
import numpy as np
import ml_dtypes
import concourse.bass as bass
import concourse.mybir as mybir
from concourse.bass_utils import run_bass_kernel_spmd

F32 = mybir.dt.float32
BF16 = mybir.dt.bfloat16
AF = mybir.ActivationFunctionType
ALU = mybir.AluOpType

NCORES = 8
B, T, D = 16, 2048, 1024
BC = B // NCORES
NT = BC * T
KT = D // 128
NTT = NT // 128
NH, DH = 16, 64

ENGINES = ["sync", "tensor", "vector", "scalar", "gpsimd"]
ESEM = {"sync": "sy", "tensor": "pe", "vector": "ve", "scalar": "se", "gpsimd": "ge"}


def tile_k(w):
    K, M = w.shape
    assert K % 128 == 0
    return np.ascontiguousarray(
        w.reshape(K // 128, 128, M).transpose(1, 0, 2).reshape(128, (K // 128) * M))


def tile_b(b):
    M = b.shape[0]
    assert M % 128 == 0
    return np.ascontiguousarray(b.reshape(M // 128, 128).T)


def rope_tables():
    j = np.arange(0, DH, 2, dtype=np.float64) / DH
    inv = 1.0 / (10000.0 ** j)
    t = np.arange(T, dtype=np.float64)[:, None]
    fr = t * inv[None, :]
    cos1, sin1 = np.cos(fr), np.sin(fr)
    cos = np.concatenate([cos1, cos1], axis=1)
    ssin = np.concatenate([-sin1, sin1], axis=1)
    cosF = np.tile(cos, (1, NH)).astype(np.float32)
    ssinF = np.tile(ssin, (1, NH)).astype(np.float32)
    def tt(a):
        return np.ascontiguousarray(
            a.reshape(T // 128, 128, D).transpose(1, 0, 2).reshape(128, (T // 128) * D))
    return tt(cosF), tt(ssinF)


def alibi_slopes(num_heads):
    closest_pow2 = 2 ** int(math.floor(math.log2(num_heads)))
    base = 2.0 ** (-(2.0 ** (-(math.log2(closest_pow2) - 3))))
    powers = np.arange(1, closest_pow2 + 1, dtype=np.float32)
    slopes = base ** powers
    if closest_pow2 != num_heads:
        extra_base = 2.0 ** (-(2.0 ** (-(math.log2(2 * closest_pow2) - 3))))
        extra_powers = np.arange(1, 2 * (num_heads - closest_pow2) + 1, 2, dtype=np.float32)
        slopes = np.concatenate([slopes, extra_base ** extra_powers])
    return slopes[:num_heads].astype(np.float32)


def host_prepare(inputs):
    _unused = (inputs["L1_attn_hormone_vectors"], inputs["L1_ffn_gate_w"])
    g = {}
    f32 = lambda a: np.ascontiguousarray(np.asarray(a, dtype=np.float32))

    g["l0_wqkv"] = tile_k(f32(inputs["L0_attn_W_qkv"]))
    g["l0_bqkv_row"] = f32(inputs["L0_attn_b_qkv"]).reshape(1, -1)
    g["l0_wout"] = tile_k(f32(inputs["L0_attn_W_out"]))
    g["l0_bout"] = tile_b(f32(inputs["L0_attn_b_out"]))
    g["l0_w1"] = tile_k(f32(inputs["L0_ffn_w1"]))
    g["l0_b1"] = tile_b(f32(inputs["L0_ffn_b1"]))
    g["l0_w2"] = tile_k(f32(inputs["L0_ffn_w2"]))
    g["l0_b2"] = tile_b(f32(inputs["L0_ffn_b2"]))

    g["l1_norm_w"] = tile_b(f32(inputs["L1_attn_norm_w"]))
    g["l1_inproj"] = tile_k(f32(inputs["L1_attn_in_proj_w"]))
    g["l1_convw"] = np.ascontiguousarray(
        f32(inputs["L1_attn_conv_w"]).reshape(16, 128, 4).transpose(1, 0, 2).reshape(128, 64))
    g["l1_convb"] = tile_b(f32(inputs["L1_attn_conv_b"]))
    xp = f32(inputs["L1_attn_x_proj_w"])
    g["l1_xp_dt"] = tile_k(xp[:, :64])
    g["l1_xp_b"] = tile_k(xp[:, 64:80])
    g["l1_xp_c"] = tile_k(xp[:, 80:96])
    g["l1_dtw"] = f32(inputs["L1_attn_dt_proj_w"])
    g["l1_dtb"] = tile_b(f32(inputs["L1_attn_dt_proj_b"]))
    g["l1_alog"] = np.ascontiguousarray(
        f32(inputs["L1_attn_A_log"]).reshape(16, 128, 16).transpose(1, 0, 2).reshape(128, 256))
    g["l1_d"] = tile_b(f32(inputs["L1_attn_D"]))
    g["l1_outproj"] = tile_k(f32(inputs["L1_attn_out_proj_w"]))
    g["l1f_lnsc"] = tile_b(f32(inputs["L1_ffn_ln_scale"]))
    g["l1f_lnbi"] = tile_b(f32(inputs["L1_ffn_ln_bias"]))
    g["l1f_attnw"] = tile_k(f32(inputs["L1_ffn_attn_w"]).T)
    g["l1f_attnb"] = tile_b(f32(inputs["L1_ffn_attn_b"]))
    g["l1f_nextw"] = tile_k(f32(inputs["L1_ffn_next_w"]).T)
    g["l1f_nextb"] = tile_b(f32(inputs["L1_ffn_next_b"]))

    g["l2_wqkv"] = tile_k(f32(inputs["L2_attn_w_qkv"]).T)
    g["l2_wout"] = tile_k(f32(inputs["L2_attn_w_out"]).T)
    g["l2_wg"] = tile_k(f32(inputs["L2_ffn_Wg"]))
    g["l2_bg_row"] = f32(inputs["L2_ffn_bg"]).reshape(1, -1)
    g["l2_bg_col"] = f32(inputs["L2_ffn_bg"]).reshape(-1, 1)
    for e in range(8):
        g[f"l2_w1_{e}"] = tile_k(f32(inputs["L2_ffn_W1"][e]))
        g[f"l2_b1_{e}"] = tile_b(f32(inputs["L2_ffn_b1"][e]))
        g[f"l2_w2_{e}"] = tile_k(f32(inputs["L2_ffn_W2"][e]))
        g[f"l2_b2_{e}"] = tile_b(f32(inputs["L2_ffn_b2"][e]))

    g["l3_wqkv"] = tile_k(f32(inputs["L3_attn_W_qkv"]))
    g["l3_wout"] = tile_k(f32(inputs["L3_attn_W_out"]))
    g["l3_w1"] = tile_k(f32(inputs["L3_ffn_w1_w"]).T)
    g["l3_s1"] = tile_b(f32(inputs["L3_ffn_w1_s"]))
    g["l3_w2"] = tile_k(f32(inputs["L3_ffn_w2_w"]).T)
    g["l3_s2"] = tile_b(f32(inputs["L3_ffn_w2_s"]))
    g["l3_h1"] = tile_k(f32(inputs["L3_ffn_h1_w"]).T)
    g["l3_sh1"] = tile_b(f32(inputs["L3_ffn_h1_s"]))
    g["l3_h2"] = tile_k(f32(inputs["L3_ffn_h2_w"]).T)
    g["l3_sh2"] = tile_b(f32(inputs["L3_ffn_h2_s"]))
    g["l3_w3"] = tile_k(f32(inputs["L3_ffn_w3_w"]).T)
    g["l3_s3"] = tile_b(f32(inputs["L3_ffn_w3_s"]))

    g["identbf"] = np.eye(128, dtype=np.float32).astype(ml_dtypes.bfloat16)
    g["onescol_bf"] = np.ones((128, 1), dtype=np.float32).astype(ml_dtypes.bfloat16)
    g["ones65"] = np.ones((65, 128), dtype=np.float32)
    g["miscrow"] = np.array([[1e-5, 1e-8, 0.0, 0.0]], dtype=np.float32)
    g["onecol"] = np.ones((128, 1), dtype=np.float32)
    g["identf"] = np.eye(128, dtype=np.float32)
    g["cosT"], g["ssinT"] = rope_tables()
    cc = np.arange(1536)[None, :] - 512
    pp = np.arange(128)[:, None]
    g["mask_l2"] = (pp <= cc).astype(np.float32).astype(ml_dtypes.bfloat16)
    cc3 = np.arange(3072)[None, :] - 384.0
    rel3 = cc3 - pp
    g["rel_l3"] = np.maximum(rel3, -1.0).astype(np.float32)
    g["relmask_l3"] = (rel3 >= 0).astype(np.float32).astype(ml_dtypes.bfloat16)

    x = np.asarray(inputs["L0_attn_x"], dtype=np.float32)
    per_core = []
    for c in range(NCORES):
        m = dict(g)
        xc = x[BC * c:BC * (c + 1)].reshape(NT, D).T
        m["x_t"] = tile_k(np.ascontiguousarray(xc))
        for i, e in enumerate([(BC * c) % 8, (BC * c + 1) % 8]):
            m[f"l1f_w1_{i}"] = tile_k(np.asarray(inputs["L1_ffn_w1"][e], dtype=np.float32))
            m[f"l1f_b1_{i}"] = tile_b(np.asarray(inputs["L1_ffn_b1"][e], dtype=np.float32))
            m[f"l1f_w2_{i}"] = tile_k(np.asarray(inputs["L1_ffn_w2"][e], dtype=np.float32))
            m[f"l1f_b2_{i}"] = tile_b(np.asarray(inputs["L1_ffn_b2"][e], dtype=np.float32))
        per_core.append(m)
    return per_core


class Prog:

    def __init__(self, nc, stack):
        self.nc = nc
        self.stack = stack
        self.plan = {e: [] for e in ENGINES}
        self.sems = {}
        self.cnt = {}
        self.last = {e: None for e in ENGINES}
        self.pending = {e: [] for e in ENGINES}
        for s in ["sy", "pe", "ve", "se", "ge"]:
            self.sems[s] = stack.enter_context(nc.semaphore(s))
            self.cnt[s] = 0

    def op(self, eng, fn, waits=(), inc=False, dma=False, mode=None):
        if eng == "tensor" and mode is not None:
            if getattr(self, "pe_mode", None) is not None and self.pe_mode != mode:
                self.plan["tensor"].append({"waits": [], "fn": lambda e: e.drain(), "incs": []})
            self.pe_mode = mode
        wl = [w for w in waits if w is not None]
        item = {"waits": list(self.pending[eng]) + wl, "fn": fn, "incs": []}
        self.pending[eng] = []
        tok = None
        if inc:
            s = ESEM[eng]
            n = 16 if dma else 1
            self.cnt[s] += n
            item["incs"].append((s, n))
            tok = (s, self.cnt[s])
        self.plan[eng].append(item)
        self.last[eng] = item
        return tok

    def dma(self, eng, out, in_, waits=()):
        return self.op(eng, lambda e, o=out, i=in_: e.dma_start(out=o, in_=i),
                       waits=waits, inc=True, dma=True)

    def barrier(self):
        for e in ENGINES:
            it = self.last[e]
            if it is None:
                continue
            if not it["incs"]:
                s = ESEM[e]
                self.cnt[s] += 1
                it["incs"].append((s, 1))
            self.last[e] = None
        snap = [(s, self.cnt[s]) for s in ["sy", "pe", "ve", "se", "ge"] if self.cnt[s]]
        for e in ENGINES:
            self.pending[e] = list(snap)

    def build(self):
        with self.nc.Block() as block:
            for eng in ENGINES:
                items = self.plan[eng]
                if not items:
                    continue
                dec = getattr(block, eng)

                def make(items):
                    def body(e):
                        for it in items:
                            for s, v in it["waits"]:
                                e.wait_ge(self.sems[s], v)
                            r = it["fn"](e)
                            for s, n in it["incs"]:
                                r.then_inc(self.sems[s], n)
                    return body
                dec(make(items))
        return self.nc


class Builder:
    def __init__(self, layers=(0, 1, 2, 3), taps=()):
        self.layers = list(layers)
        self.taps = list(taps)
        self.nc = bass.Bass("TRN2", target_bir_lowering=False)
        self.stack = contextlib.ExitStack()
        self.p = Prog(self.nc, self.stack)
        self.dram = {}

    def din(self, name, shape, dtype=F32):
        self.dram[name] = self.nc.dram_tensor(name, list(shape), dtype, kind="ExternalInput").ap()
        return self.dram[name]

    def dout(self, name, shape, dtype=F32):
        self.dram[name] = self.nc.dram_tensor(name, list(shape), dtype, kind="ExternalOutput").ap()
        return self.dram[name]

    def dtmp(self, name, shape, dtype=F32):
        self.dram[name] = self.nc.dram_tensor(name, list(shape), dtype, kind="Internal").ap()
        return self.dram[name]

    def sbuf(self, name, shape, dtype=F32, stack=None):
        self._uid = getattr(self, "_uid", 0) + 1
        return (stack or self.stack).enter_context(
            self.nc.sbuf_tensor(f"{name}_{self._uid}", list(shape), dtype))

    def psum(self, name, shape, dtype=F32, stack=None):
        self._uid = getattr(self, "_uid", 0) + 1
        return (stack or self.stack).enter_context(
            self.nc.psum_tensor(f"{name}_{self._uid}", list(shape), dtype))

    def achunk(self, name, c0, w):
        return self.dram[name].rearrange("p (k t) -> p k t", k=KT)[:, :, c0:c0 + w]

    def declare(self):
        din = self.din
        din("x_t", (128, KT * NT))
        self.dout("out_t", (128, KT * NT))
        self.dtmp("xnt", (128, KT * NT), BF16)
        self.dtmp("xnf", (128, KT * NT))
        self.dtmp("smix", (128, KT * NT))
        self.dtmp("gffn", (128, KT * NT))
        self.dtmp("res1", (128, KT * NT))
        self.dtmp("res2", (128, KT * NT))
        self.dtmp("qt", (128, BC * 16 * T), BF16)
        self.dtmp("kt_", (128, BC * 8 * T), BF16)
        self.dtmp("vt", (128, BC * 16 * NH * DH), BF16)
        self.dtmp("hbig1", (128, 32 * NT), BF16)
        self.dtmp("hbig2", (128, 32 * NT), BF16)
        self.dtmp("selt", (8, NT))
        self.dtmp("avnt", (128, 16 * T), BF16)
        self.dtmp("acct", (128, KT * NT))
        if 1 in self.layers:
            self.dtmp("hmt", (128, KT * NT), BF16)
            self.dtmp("ut", (128, 16 * NT))
            self.dtmp("zt", (128, 16 * NT), BF16)
            self.dtmp("ucf", (128, 16 * NT))
            self.dtmp("ucb", (128, 16 * NT), BF16)
            self.dtmp("dtt", (128, 16 * NT))
            self.dtmp("dub", (128, 16 * NT), BF16)
            self.dtmp("bt_m", (16, NT))
            self.dtmp("ct_m", (16, NT))
            self.dtmp("yt", (128, 16 * NT), BF16)
            self.dtmp("yacct", (128, 16 * T), BF16)
            self.dtmp("xn2t", (128, KT * NT), BF16)
            self.dtmp("xat", (128, KT * NT), BF16)
            self.dtmp("eot", (128, KT * NT), BF16)

        din("identbf", (128, 128), BF16)
        din("onescol_bf", (128, 1), BF16)
        din("ones65", (65, 128))
        din("miscrow", (1, 4))
        din("onecol", (128, 1))
        din("identf", (128, 128))
        if 0 in self.layers:
            din("l0_wqkv", (128, KT * 3072)); din("l0_bqkv_row", (1, 3072))
            din("l0_wout", (128, KT * 1024)); din("l0_bout", (128, 8))
            din("l0_w1", (128, KT * 4096)); din("l0_b1", (128, 32))
            din("l0_w2", (128, 32 * 1024)); din("l0_b2", (128, 8))
            din("cosT", (128, 16 * 1024)); din("ssinT", (128, 16 * 1024))
        if 1 in self.layers:
            din("l1_norm_w", (128, 8)); din("l1_inproj", (128, KT * 4096))
            din("l1_convw", (128, 64)); din("l1_convb", (128, 16))
            din("l1_xp_dt", (128, 16 * 64)); din("l1_xp_b", (128, 16 * 16)); din("l1_xp_c", (128, 16 * 16))
            din("l1_dtw", (64, 2048)); din("l1_dtb", (128, 16))
            din("l1_alog", (128, 256)); din("l1_d", (128, 16))
            din("l1_outproj", (128, 16 * 1024))
            din("l1f_lnsc", (128, 8)); din("l1f_lnbi", (128, 8))
            din("l1f_attnw", (128, KT * 1024)); din("l1f_attnb", (128, 8))
            din("l1f_nextw", (128, KT * 1024)); din("l1f_nextb", (128, 8))
            for i in range(2):
                din(f"l1f_w1_{i}", (128, KT * 4096)); din(f"l1f_b1_{i}", (128, 32))
                din(f"l1f_w2_{i}", (128, 32 * 1024)); din(f"l1f_b2_{i}", (128, 8))
        if 2 in self.layers:
            din("l2_wqkv", (128, KT * 3072)); din("l2_wout", (128, KT * 1024))
            din("l2_wg", (128, KT * 8)); din("l2_bg_row", (1, 8)); din("l2_bg_col", (8, 1))
            din("mask_l2", (128, 1536), BF16)
            for e in range(8):
                din(f"l2_w1_{e}", (128, KT * 4096)); din(f"l2_b1_{e}", (128, 32))
                din(f"l2_w2_{e}", (128, 32 * 1024)); din(f"l2_b2_{e}", (128, 8))
        if 3 in self.layers:
            din("l3_wqkv", (128, KT * 4096))
            din("l3_wout", (128, 16 * 1024))
            din("rel_l3", (128, 3072)); din("relmask_l3", (128, 3072), BF16)
            din("l3_w1", (128, KT * 4096)); din("l3_s1", (128, 32))
            din("l3_w2", (128, KT * 4096)); din("l3_s2", (128, 32))
            din("l3_h1", (128, 32 * 4096)); din("l3_sh1", (128, 32))
            din("l3_h2", (128, 32 * 4096)); din("l3_sh2", (128, 32))
            din("l3_w3", (128, 32 * 1024)); din("l3_s3", (128, 8))

        self.c_ident = self.sbuf("c_ident", (128, 128), BF16)
        self.c_ones = self.sbuf("c_ones", (128, 1), BF16)
        self.c_ones65 = self.sbuf("c_ones65", (65, 128))
        self.c_misc = self.sbuf("c_misc", (1, 4))
        self.slackrow = self.sbuf("c_slack", (1, 128))
        self.c_onecol = self.sbuf("c_onecol", (128, 1))
        self.c_ident_f = self.sbuf("c_ident_f", (128, 128))

    @staticmethod
    def mmmode(k, m):
        r = lambda v: 32 if v <= 32 else (64 if v <= 64 else 128)
        return (r(k), r(m), False)

    def load_consts(self):
        p = self.p
        p.dma("sync", self.c_ident[:], self.dram["identbf"])
        p.dma("sync", self.c_ones[:], self.dram["onescol_bf"])
        p.dma("sync", self.c_ones65[:], self.dram["ones65"])
        p.dma("sync", self.c_misc[:], self.dram["miscrow"])
        p.dma("sync", self.c_onecol[:], self.dram["onecol"])
        p.dma("sync", self.c_ident_f[:], self.dram["identf"])
        p.op("vector", lambda e: e.memset(self.slackrow[:], 0.0))
        p.barrier()

    def wload(self, dst_sb, src_name, nelem, stage=None, src_off=0, after=None):
        p = self.p
        tl = p.dma("gpsimd", dst_sb[:, 0:nelem],
                   self.dram[src_name][:, src_off:src_off + nelem], waits=[after])
        return self.slack(tl)

    def slack(self, tok, n=2):
        p = self.p
        t = tok
        for i in range(n):
            t = p.op("scalar", lambda e: e.copy(self.slackrow[0:1, 0:64], self.slackrow[0:1, 64:128]),
                     waits=[t], inc=True)
        return t

    def colload(self, name, shape=None, stack=None, dtype=F32):
        ap = self.dram[name]
        shape = shape or ap.shape
        t = self.sbuf("cl_" + name, list(shape), dtype, stack or self.stack)
        tok = self.p.dma("sync", t[:], ap)
        return t, tok

    def ph_ln(self, src_name, want_f32=False):
        p = self.p
        st = contextlib.ExitStack()
        W = 512
        NCH = NT // W
        XCH = [self.sbuf(f"ln_x{i}", (128, KT * W), F32, st) for i in range(2)]
        XB = [self.sbuf(f"ln_xb{i}", (128, KT * W), BF16, st) for i in range(2)]
        XSQ = [self.sbuf(f"ln_xsq{i}", (128, KT * W), BF16, st) for i in range(2)]
        ROWS = [self.sbuf(f"ln_rows{i}", (1, W * 5), F32, st) for i in range(2)]
        RSB = [self.sbuf(f"ln_rsb{i}", (128, W), F32, st) for i in range(2)]
        MRSB = [self.sbuf(f"ln_mrsb{i}", (128, W), F32, st) for i in range(2)]
        XN = [self.sbuf(f"ln_xn{i}", (128, KT * W), F32, st) for i in range(2)]
        XNB = [self.sbuf(f"ln_xnb{i}", (128, KT * W), BF16, st) for i in range(2)]
        PS = [self.psum(f"ln_ps{i}", (1, W), F32, st) for i in range(2)]
        PQ = [self.psum(f"ln_pq{i}", (1, W), F32, st) for i in range(2)]
        PR1 = [self.psum(f"ln_pr1{i}", (128, W), F32, st) for i in range(2)]
        PR2 = [self.psum(f"ln_pr2{i}", (128, W), F32, st) for i in range(2)]
        dep = {}

        for c in range(NCH):
            j = c % 2
            d = dep.get(j, {})
            tl = p.dma("sync", XCH[j][:], self.achunk(src_name, c * W, W),
                       waits=[d.get("xch")])
            txb = p.op("vector", lambda e, j=j: e.tensor_copy(XB[j][:], XCH[j][:]),
                       waits=[tl, d.get("xb")], inc=True)
            txb = self.slack(txb)
            tsq = p.op("scalar", lambda e, j=j: e.activation(XSQ[j][:], XCH[j][:], AF.Square),
                       waits=[tl, d.get("xsq")], inc=True)
            tsq = self.slack(tsq, 1)
            for kt in range(KT):
                p.op("tensor", lambda e, j=j, kt=kt: e.matmul(
                    PS[j][:], self.c_ones[:], XB[j][:, kt * W:(kt + 1) * W],
                    start=(kt == 0), stop=(kt == KT - 1)),
                    waits=[txb, d.get("ps")] if kt == 0 else (), mode=(128, 32, False))
            tstat = None
            for kt in range(KT):
                tstat = p.op("tensor", lambda e, j=j, kt=kt: e.matmul(
                    PQ[j][:], self.c_ones[:], XSQ[j][:, kt * W:(kt + 1) * W],
                    start=(kt == 0), stop=(kt == KT - 1)),
                    waits=[tsq, d.get("pq")] if kt == 0 else (),
                    inc=(kt == KT - 1), mode=(128, 32, False))
            tstat = p.op("tensor", lambda e: e.drain(), inc=True)
            R = ROWS[j]
            mrow, qrow, vrow = R[:, 0:W], R[:, W:2 * W], R[:, 2 * W:3 * W]
            rrow, mrrow = R[:, 3 * W:4 * W], R[:, 4 * W:5 * W]
            tmrow = p.op("vector", lambda e, mrow=mrow, j=j: e.tensor_scalar(
                mrow, PS[j][:], 1.0 / D, None, ALU.mult), waits=[tstat, d.get("rows")], inc=True)
            tps = p.op("vector", lambda e, qrow=qrow, j=j: e.tensor_scalar(
                qrow, PQ[j][:], 1.0 / D, None, ALU.mult), inc=True)
            tv1_ = p.op("vector", lambda e, vrow=vrow, mrow=mrow: e.tensor_tensor(
                vrow, mrow, mrow, ALU.mult), waits=[tps], inc=True)
            tvar = p.op("vector", lambda e, vrow=vrow, qrow=qrow: e.tensor_tensor(
                vrow, qrow, vrow, ALU.subtract), waits=[tv1_], inc=True)
            p.op("scalar", lambda e, vrow=vrow: e.activation(vrow, vrow, AF.Ln, bias=self.c_misc[0:1, 0:1]),
                 waits=[tvar])
            trs = p.op("scalar", lambda e, rrow=rrow, vrow=vrow: e.activation(
                rrow, vrow, AF.Exp, scale=-0.5), inc=True)
            tmr = p.op("vector", lambda e, mrrow=mrrow, mrow=mrow, rrow=rrow: e.tensor_tensor(
                mrrow, mrow, rrow, ALU.mult), waits=[trs, tmrow], inc=True)
            tmr = self.slack(tmr)
            p.op("tensor", lambda e, j=j, rrow=rrow: e.matmul(
                PR1[j][:], self.c_ones65[0:1, :], rrow, start=True, stop=True),
                waits=[tmr, d.get("pr")], mode=(32, 128, False))
            p.op("tensor", lambda e, j=j, mrrow=mrrow: e.matmul(
                PR2[j][:], self.c_ones65[0:1, :], mrrow, start=True, stop=True),
                mode=(32, 128, False))
            trep = p.op("tensor", lambda e: e.drain(), inc=True)
            p.op("scalar", lambda e, j=j: e.copy(RSB[j][:], PR1[j][:]),
                 waits=[trep, d.get("rsb")])
            tevr = p.op("scalar", lambda e, j=j: e.copy(MRSB[j][:], PR2[j][:]), inc=True)
            bview = lambda t: t[:].unsqueeze(1).broadcast_to([128, KT, W])
            xview = lambda t: t[:].rearrange("p (k t) -> p k t", k=KT)
            tmul = p.op("vector", lambda e, j=j, bview=bview, xview=xview: e.tensor_tensor(
                xview(XN[j]), xview(XCH[j]), bview(RSB[j]), ALU.mult),
                waits=[tevr, d.get("xn")], inc=True)
            if want_f32:
                txn = p.op("vector", lambda e, j=j, bview=bview, xview=xview: e.tensor_tensor(
                    xview(XN[j]), xview(XN[j]), bview(MRSB[j]), ALU.subtract),
                    waits=[tmul], inc=True)
                tsf = p.dma("sync", self.achunk("xnf", c * W, W), XN[j][:], waits=[txn])
                txnb = p.op("vector", lambda e, j=j: e.tensor_copy(XNB[j][:], XN[j][:]),
                            waits=[txn, d.get("xnb")], inc=True)
                tsb = p.dma("sync", self.achunk("xnt", c * W, W), XNB[j][:], waits=[txnb])
                xn_free = (txnb, tsf)
            else:
                txnb = p.op("vector", lambda e, j=j, bview=bview, xview=xview: e.tensor_tensor(
                    xview(XNB[j]), xview(XN[j]), bview(MRSB[j]), ALU.subtract),
                    waits=[tmul, d.get("xnb")], inc=True)
                tsb = p.dma("sync", self.achunk("xnt", c * W, W), XNB[j][:], waits=[txnb])
                xn_free = (txnb, None)
            dep[j] = {
                "xch": tmul,
                "xb": tstat, "xsq": tstat,
                "ps": tps, "pq": tps,
                "rows": trep, "pr": tevr, "rsb": xn_free[0],
                "xn": xn_free[0] if xn_free[1] is None else xn_free[1],
                "xnb": tsb,
            }
        p.barrier()
        st.close()

    def ph_combine(self, xsrc, dst):
        p = self.p
        st = contextlib.ExitStack()
        W = 512
        XC = [self.sbuf(f"cb_x{i}", (128, KT * W), F32, st) for i in range(2)]
        SC = [self.sbuf(f"cb_s{i}", (128, KT * W), F32, st) for i in range(2)]
        GC = [self.sbuf(f"cb_g{i}", (128, KT * W), F32, st) for i in range(2)]
        OC = [self.sbuf(f"cb_o{i}", (128, KT * W), F32, st) for i in range(2)]
        dep = {}
        for c in range(NT // W):
            j = c % 2
            d = dep.get(j, {})
            tx = p.dma("sync", XC[j][:], self.achunk(xsrc, c * W, W), waits=[d.get("x")])
            ts = p.dma("gpsimd", SC[j][:], self.achunk("smix", c * W, W), waits=[d.get("s")])
            tg = p.dma("gpsimd", GC[j][:], self.achunk("gffn", c * W, W), waits=[d.get("s")])
            tm = p.op("vector", lambda e, j=j: e.tensor_tensor(
                SC[j][:], SC[j][:], GC[j][:], ALU.mult), waits=[ts, tg], inc=True)
            to = p.op("vector", lambda e, j=j: e.tensor_tensor(
                OC[j][:], XC[j][:], SC[j][:], ALU.add), waits=[tm, tx, d.get("o")], inc=True)
            tw = p.dma("sync", self.achunk(dst, c * W, W), OC[j][:], waits=[to])
            dep[j] = {"x": to, "s": to, "o": tw}
        p.barrier()
        st.close()

    def ph_l0_qkv(self):
        p = self.p
        st = contextlib.ExitStack()
        WQKV = self.sbuf("q0_w", (128, KT * 3072), BF16, st)
        twq = self.wload(WQKV, "l0_wqkv", KT * 3072)
        BQROW = self.sbuf("q0_brow", (1, 3072), F32, st)
        BREP = self.sbuf("q0_brep", (128, 3072), F32, st)
        PB = self.psum("q0_pb", (128, 512), F32, st)
        tb = p.dma("sync", BQROW[:], self.dram["l0_bqkv_row"])
        tev = None
        for fc in range(6):
            p.op("tensor", lambda e, fc=fc: e.matmul(
                PB[:], self.c_ones65[0:1, :], BQROW[:, fc * 512:(fc + 1) * 512],
                start=True, stop=True), waits=[self.slack(tb) if fc == 0 else None, tev],
                mode=(32, 128, False))
            tmm = p.op("tensor", lambda e: e.drain(), inc=True)
            tev = p.op("scalar", lambda e, fc=fc: e.copy(
                BREP[:, fc * 512:(fc + 1) * 512], PB[:]), waits=[tmm], inc=True)
        p.barrier()

        XNS = [self.sbuf(f"q0_xn{i}", (128, KT * 128), BF16, st) for i in range(2)]
        COSS = [self.sbuf(f"q0_cos{i}", (128, 1024), F32, st) for i in range(2)]
        SSINS = [self.sbuf(f"q0_sin{i}", (128, 1024), F32, st) for i in range(2)]
        QKVB = [self.sbuf(f"q0_qkv{i}", (128, 3072), F32, st) for i in range(2)]
        RA = [self.sbuf(f"q0_ra{i}", (128, 1024), F32, st) for i in range(2)]
        RB = [self.sbuf(f"q0_rb{i}", (128, 1024), F32, st) for i in range(2)]
        QKB = [self.sbuf(f"q0_qkb{i}", (128, 2048), BF16, st) for i in range(2)]
        VB = [self.sbuf(f"q0_vb{i}", (128, 1024), BF16, st) for i in range(2)]
        TST = [self.sbuf(f"q0_tst{i}", (128, 8 * 128), BF16, st) for i in range(4)]
        PMM = [self.psum(f"q0_pm{i}", (128, 512), F32, st) for i in range(2)]
        PTR = [self.psum(f"q0_pt{i}", (128, 128), BF16, st) for i in range(2)]
        dep = {}
        tptr = [None, None]
        for mt in range(NTT):
            j = mt % 2
            d = dep.get(j, {})
            bb, tt = mt // 16, mt % 16
            p.dma("sync", XNS[j][:],
                  self.dram["xnt"].rearrange("p (k t) -> p k t", k=KT)[:, :, mt * 128:(mt + 1) * 128],
                  waits=[d.get("xns")])
            tx = p.dma("sync", XNS[j][:],
                       self.dram["xnt"].rearrange("p (k t) -> p k t", k=KT)[:, :, mt * 128:(mt + 1) * 128])
            tc_ = p.dma("sync", COSS[j][:], self.dram["cosT"][:, tt * 1024:(tt + 1) * 1024],
                        waits=[d.get("cos")])
            tsn = p.dma("sync", SSINS[j][:], self.dram["ssinT"][:, tt * 1024:(tt + 1) * 1024],
                        waits=[d.get("cos")])
            tqkv = []
            tgrp = []
            def emit_evac(fc, safe):
                pj = fc % 2
                tq = p.op("vector", lambda e, j=j, pj=pj, fc=fc: e.tensor_tensor(
                    QKVB[j][:, fc * 512:(fc + 1) * 512], PMM[pj][:],
                    BREP[:, fc * 512:(fc + 1) * 512], ALU.add),
                    waits=[safe, d.get("qkvb") if fc <= 1 else None], inc=True)
                tqkv.append(tq)
                dep.setdefault(j, {})
                dep[j] = {**dep.get(j, {}), f"pm{pj}": tq}
            for fc in range(6):
                pj = fc % 2
                tmm = None
                for kt in range(KT):
                    w = []
                    if kt == 0:
                        w = [tx, twq if mt == 0 else None,
                             d.get(f"pm{pj}") if fc < 2 else None,
                             tqkv[fc - 2] if fc >= 2 and len(tqkv) > fc - 2 else None]
                    tmm = p.op("tensor", lambda e, j=j, pj=pj, fc=fc, kt=kt: e.matmul(
                        PMM[pj][:], XNS[j][:, kt * 128:(kt + 1) * 128],
                        WQKV[:, kt * 3072 + fc * 512: kt * 3072 + fc * 512 + 512],
                        start=(kt == 0), stop=(kt == KT - 1)),
                        waits=w, inc=(kt == KT - 1), mode=(128, 128, False))
                tgrp.append(tmm)
                if fc >= 1:
                    emit_evac(fc - 1, tmm)
            tdr = p.op("tensor", lambda e: e.drain(), inc=True)
            emit_evac(5, tdr)
            tqkv_all = tqkv[-1]
            rope_toks = []
            for qi in range(2):
                off = qi * 1024
                hv = lambda t, a, b: t.rearrange("p (h i) -> p h i", h=16)[:, :, a:b]
                tra = p.op("vector", lambda e, j=j, off=off: e.tensor_tensor(
                    RA[j][:], QKVB[j][:, off:off + 1024], COSS[j][:], ALU.mult),
                    waits=[tqkv[qi * 2 + 1], tc_, d.get("ra")], inc=True)
                p.op("vector", lambda e, j=j, off=off, hv=hv: e.tensor_tensor(
                    hv(RB[j][:], 0, 32), hv(QKVB[j][:, off:off + 1024], 32, 64),
                    hv(SSINS[j][:], 0, 32), ALU.mult), waits=[tsn])
                trb = p.op("vector", lambda e, j=j, off=off, hv=hv: e.tensor_tensor(
                    hv(RB[j][:], 32, 64), hv(QKVB[j][:, off:off + 1024], 0, 32),
                    hv(SSINS[j][:], 32, 64), ALU.mult), inc=True)
                tr = p.op("vector", lambda e, j=j, qi=qi: e.tensor_tensor(
                    QKB[j][:, qi * 1024:(qi + 1) * 1024], RA[j][:], RB[j][:], ALU.add),
                    waits=[tra, trb, d.get("qkb")], inc=True)
                rope_toks.append(tr)
            tvb = p.op("vector", lambda e, j=j: e.tensor_copy(VB[j][:], QKVB[j][:, 2048:3072]),
                       waits=[tqkv_all, d.get("vb")], inc=True)
            tvs = p.dma("gpsimd", self.dram["vt"][:, (bb * 16 + tt) * 1024:(bb * 16 + tt + 1) * 1024],
                        VB[j][:], waits=[tvb])
            dep[j] = {**dep.get(j, {}), "vb": tvs}
            tst_free = d.get("tst", [None, None])
            store_toks = []
            for qi, dname in ((0, "qt"), (1, "kt_")):
                tb_ = TST[j * 2 + qi]
                tev2 = None
                for hp in range(8):
                    pj2 = hp % 2
                    p.op("tensor", lambda e, j=j, qi=qi, hp=hp, pj2=pj2: e.transpose(
                        PTR[pj2][:], QKB[j][:, qi * 1024 + hp * 128: qi * 1024 + (hp + 1) * 128],
                        self.c_ident[:]),
                        waits=[self.slack(rope_toks[qi], 1) if hp == 0 else rope_toks[qi],
                               tptr[pj2], tst_free[qi] if hp == 0 else None],
                        mode=(128, 128, True))
                    ttr = p.op("tensor", lambda e: e.drain(), inc=True)
                    tev2 = p.op("scalar", lambda e, tb_=tb_, hp=hp, pj2=pj2: e.copy(
                        tb_[:, hp * 128:(hp + 1) * 128], PTR[pj2][:]),
                        waits=[ttr], inc=True)
                    tptr[pj2] = tev2
                dst = self.dram[dname].rearrange("p (b h t) -> p b h t", b=BC, h=16 if dname == "qt" else 8)[
                    :, bb, 0:8, tt * 128:(tt + 1) * 128]
                tst_store = p.dma("scalar", dst, tb_[:].rearrange("p (h t) -> p h t", h=8),
                                  waits=[tev2])
                store_toks.append(tst_store)
            dep[j] = {**dep.get(j, {}),
                      "xns": tqkv_all, "cos": rope_toks[1],
                      "qkvb": tvb, "ra": rope_toks[1], "qkb": store_toks[-1],
                      "tst": store_toks}
        p.barrier()
        st.close()

    def ph_attn(self, layer):
        p = self.p
        st = contextlib.ExitStack()
        causal = layer in (2, 3)
        alibi = layer == 3
        nqh = 32 if layer == 3 else 16
        NHP = nqh // 2
        KTILES = T // 128
        QC = T // 512
        scale = 1.0 / math.sqrt(DH)

        KTB = self.sbuf("at_kt", (128, T), BF16, st)
        QTB = self.sbuf("at_qt", (128, T), BF16, st)
        VS = [self.sbuf(f"at_vs{i}", (128, KTILES * 65), BF16, st) for i in range(2)]
        ES = [self.sbuf(f"at_es{i}", (128, KTILES * 512), BF16, st) for i in range(2)]
        AVU = [self.sbuf(f"at_avu{i}", (64, QC * 512), F32, st) for i in range(2)]
        AVSTG = [self.sbuf(f"at_avstg{i}", (64, 512), BF16, st) for i in range(2)]
        DNR = self.sbuf("at_dnr", (65, 2 * T), F32, st)
        PS_ = [self.psum(f"at_ps{i}", (128, 512), F32, st) for i in range(4)]
        PAV = [self.psum(f"at_pav{i}", (128, 512), F32, st) for i in range(2)]
        PRR = [self.psum(f"at_prr{i}", (128, 512), F32, st) for i in range(2)]
        MSK = ALB = None
        talb = None
        if layer == 2:
            MSK = self.sbuf("at_msk", (128, 1536), BF16, st)
            p.dma("sync", MSK[:], self.dram["mask_l2"])
        if alibi:
            REL = self.sbuf("at_rel", (128, 3072), F32, st)
            RMS = self.sbuf("at_rms", (128, 3072), BF16, st)
            ALBS = [self.sbuf(f"at_alb{i}", (128, 3072), BF16, st) for i in range(2)]
            ALBF = self.sbuf("at_albf", (128, 3072), F32, st)
            p.dma("sync", REL[:], self.dram["rel_l3"])
            p.dma("sync", RMS[:], self.dram["relmask_l3"])
            slopes = alibi_slopes(32)
        tv1 = [None, None]
        for i in range(2):
            tv1[i] = p.op("vector", lambda e, i=i: e.memset(
                VS[i][:].rearrange("p (k c) -> p k c", k=KTILES)[:, :, 64:65], 1.0), inc=True)
        p.barrier()

        wo_stage = contextlib.ExitStack()
        self._prep_outproj(layer, wo_stage, PS_)
        p.barrier()

        dep = {}
        for bb in range(BC):
            for hp in range(NHP):
                d = dep
                kv_hp = hp if layer != 3 else None
                if layer == 3:
                    kvh = hp
                    ksrc = self.dram["kt_"].rearrange("p (b h t) -> p b h t", b=BC, h=8)[
                        :, bb, kvh // 2, :]
                    tk1 = p.dma("sync", KTB[0:64, :],
                                ksrc.tensor[(kvh % 2) * 64:(kvh % 2) * 64 + 64, :]
                                if False else
                                self.dram["kt_"].rearrange("(z q) (b h t) -> z q b h t", z=2, b=BC, h=8)[
                                    kvh % 2, :, bb, kvh // 2, :],
                                waits=[d.get("ktb")])
                    tk2 = p.dma("sync", KTB[64:128, :],
                                self.dram["kt_"].rearrange("(z q) (b h t) -> z q b h t", z=2, b=BC, h=8)[
                                    kvh % 2, :, bb, kvh // 2, :],
                                waits=[d.get("ktb")])
                    tkq = p.dma("sync", QTB[:],
                                self.dram["qt"].rearrange("p (b h t) -> p b h t", b=BC, h=16)[:, bb, hp, :],
                                waits=[d.get("qtb")])
                    tks = [self.slack(tk2), self.slack(tkq, 1)]
                else:
                    tk = p.dma("sync", KTB[:],
                               self.dram["kt_"].rearrange("p (b h t) -> p b h t", b=BC, h=8)[:, bb, hp, :],
                               waits=[d.get("ktb")])
                    tq = p.dma("sync", QTB[:],
                               self.dram["qt"].rearrange("p (b h t) -> p b h t", b=BC, h=16)[:, bb, hp, :],
                               waits=[d.get("qtb")])
                    tks = [self.slack(tk), self.slack(tq, 1)]
                tvl = []
                for i in range(2):
                    h = hp * 2 + i
                    kvh = h // 2 if layer == 3 else h
                    src = self.dram["vt"].rearrange("p (b t h dd) -> p b t h dd", b=BC, t=16, h=NH)[
                        :, bb, :, kvh, :]
                    tvl.append(self.slack(p.dma("gpsimd",
                                     VS[i][:].rearrange("p (k c) -> p k c", k=KTILES)[:, :, 0:64],
                                     src, waits=[tv1[i], d.get(f"vs{i}")]), 1))
                if alibi:
                    ta = []
                    tprevalb = d.get("alb")
                    for i in range(2):
                        h = hp * 2 + i
                        t1 = p.op("scalar", lambda e, h=h: e.activation(
                            ALBF[:], REL[:], AF.Exp, scale=-float(slopes[h])),
                            waits=[tprevalb, ta[0] if i == 1 else None], inc=True)
                        tai = p.op("vector", lambda e, i=i: e.tensor_tensor(
                            ALBS[i][:], ALBF[:], RMS[:], ALU.mult), waits=[t1, d.get(f"albs{i}")],
                            inc=True)
                        ta.append(tai)
                    talb = ta
                es_free = d.get("es", [None, None])
                bank_free = d.get("banks", [None] * 4)
                av_done = []
                dn_done = []
                for qc in range(QC):
                    kt_hi = KTILES if not causal else (qc * 4 + 4)
                    texps = [[None] * KTILES, [None] * KTILES]
                    tmms = [[None] * KTILES, [None] * KTILES]
                    def emit_exp(kt, i, safe):
                        bi = i * 2 + kt % 2
                        texp = p.op("scalar", lambda e, i=i, kt=kt, bi=bi: e.activation(
                            ES[i][:, kt * 512:(kt + 1) * 512], PS_[bi][:], AF.Exp, scale=scale),
                            waits=[safe], inc=True)
                        texps[i][kt] = texp
                        bank_free[bi] = texp
                    for kt in range(kt_hi):
                        pj = kt % 2
                        for i in range(2):
                            bi = i * 2 + pj
                            tmms[i][kt] = p.op("tensor", lambda e, i=i, kt=kt, qc=qc, bi=bi: e.matmul(
                                PS_[bi][:], KTB[i * 64:(i + 1) * 64, kt * 128:(kt + 1) * 128],
                                QTB[i * 64:(i + 1) * 64, qc * 512:(qc + 1) * 512],
                                start=True, stop=True),
                                waits=[tks[0] if kt == 0 and qc == 0 else None,
                                       tks[-1] if kt == 0 and qc == 0 else None,
                                       bank_free[bi],
                                       es_free[i] if kt == 0 and qc == 0 else None],
                                inc=True, mode=(64, 128, False))
                        if kt >= 1:
                            emit_exp(kt - 1, 0, tmms[0][kt])
                            emit_exp(kt - 1, 1, tmms[1][kt])
                    tdr_s = p.op("tensor", lambda e: e.drain(), inc=True)
                    emit_exp(kt_hi - 1, 0, tdr_s)
                    emit_exp(kt_hi - 1, 1, tdr_s)
                    tmsk = [[None] * KTILES, [None] * KTILES]
                    if layer == 2:
                        for kt in range(qc * 4, kt_hi):
                            doff = qc * 512 - kt * 128 + 512
                            for i in range(2):
                                tmsk[i][kt] = p.op("vector", lambda e, i=i, kt=kt, doff=doff: e.tensor_tensor(
                                    ES[i][:, kt * 512:(kt + 1) * 512],
                                    ES[i][:, kt * 512:(kt + 1) * 512],
                                    MSK[:, doff:doff + 512], ALU.mult),
                                    waits=[texps[i][kt]], inc=True)
                    if alibi:
                        for kt in range(kt_hi):
                            aoff = qc * 512 - kt * 128 + 384
                            for i in range(2):
                                tmsk[i][kt] = p.op("vector", lambda e, i=i, kt=kt, aoff=aoff: e.tensor_tensor(
                                    ES[i][:, kt * 512:(kt + 1) * 512],
                                    ES[i][:, kt * 512:(kt + 1) * 512],
                                    ALBS[i][:, aoff:aoff + 512], ALU.mult),
                                    waits=[texps[i][kt], talb[i]], inc=True)
                    tavs = [None, None]
                    for i in range(2):
                        for kt in range(kt_hi):
                            tavs[i] = p.op("tensor", lambda e, i=i, kt=kt, kh=kt_hi: e.matmul(
                                PAV[i][0:65, :], VS[i][:, kt * 65:kt * 65 + 65],
                                ES[i][:, kt * 512:(kt + 1) * 512],
                                start=(kt == 0), stop=(kt == kh - 1)),
                                waits=[(tmsk[i][kt_hi - 1] or texps[i][kt_hi - 1]) if kt == 0 else None,
                                       (tmsk[i][kt] or texps[i][kt]) if kt > 0 else None,
                                       tvl[i] if kt == 0 else None,
                                       (d.get("pav") or [None, None])[i] if kt == 0 else None],
                                inc=(kt == kt_hi - 1), mode=(128, 128, False))
                    tdr_av = p.op("tensor", lambda e: e.drain(), inc=True)
                    for i in range(2):
                        safe = tavs[1] if i == 0 else tdr_av
                        tevac = p.op("vector", lambda e, i=i, qc=qc: e.tensor_copy(
                            AVU[i][:, qc * 512:(qc + 1) * 512], PAV[i][0:64, :]),
                            waits=[safe, d.get(f"avu{i}") if qc == 0 else None])
                        tdn = p.op("vector", lambda e, i=i, qc=qc: e.tensor_copy(
                            DNR[64:65, i * T + qc * 512: i * T + (qc + 1) * 512],
                            PAV[i][64:65, :]),
                            waits=[d.get("dnr") if qc == 0 and i == 0 else None], inc=True)
                        av_done.append(tdn)
                        dep.setdefault("pav", [None, None])
                        dep["pav"][i] = tdn
                    es_free = [av_done[-2], av_done[-1]]
                p.op("scalar", lambda e: e.activation(DNR[64:65, :], DNR[64:65, :], AF.Ln),
                     waits=[av_done[-1], av_done[-2]] + av_done[:2])
                trec = p.op("scalar", lambda e: e.activation(
                    DNR[64:65, :], DNR[64:65, :], AF.Exp, scale=-1.0), inc=True)
                tnorm = []
                prr_free = d.get("prr", [None, None])
                stg_free = d.get("stg", [None, None])
                for i in range(2):
                    for qc in range(QC):
                        pr = i
                        p.op("tensor", lambda e, i=i, qc=qc, pr=pr: e.matmul(
                            PRR[pr][:], self.c_ones65[64:65, :],
                            DNR[64:65, i * T + qc * 512: i * T + (qc + 1) * 512],
                            start=True, stop=True),
                            waits=[trec, prr_free[pr]], mode=(32, 128, False))
                        trr = p.op("tensor", lambda e: e.drain(), inc=True)
                        tn = p.op("vector", lambda e, i=i, qc=qc, pr=pr: e.tensor_tensor(
                            AVSTG[i][:], AVU[i][:, qc * 512:(qc + 1) * 512],
                            PRR[pr][0:64, :], ALU.mult),
                            waits=[trr, stg_free[i]], inc=True)
                        tsh = p.dma("gpsimd",
                                    self.dram["avnt"][i * 64:(i + 1) * 64,
                                                      hp * T + qc * 512: hp * T + (qc + 1) * 512],
                                    AVSTG[i][:], waits=[tn, d.get("avn") if qc == 0 else None])
                        tnorm.append(tsh)
                        prr_free[pr] = tsh
                        stg_free[i] = tsh
                dep["stg"] = stg_free
                dep.update({
                    "ktb": av_done[-1], "qtb": av_done[-1],
                    "vs0": av_done[-2], "vs1": av_done[-1],
                    "es": es_free, "banks": bank_free,
                    "dnr": trr, "prr": prr_free,
                    "avu0": tnorm[0] if False else tnorm[QC - 1],
                    "avu1": tnorm[-1],
                    "alb": talb[-1] if alibi else None,
                    "avn_parts": (dep.get("avn_parts") or []) + tnorm,
                })
            tavn = dep.pop("avn_parts")
            tout = self._attn_outproj(layer, bb, tavn, dep["banks"])
            dep["avn"] = tout
            dep["avn_parts"] = []
        p.barrier()
        wo_stage.close()
        st.close()

    def _prep_outproj(self, layer, st, PS_):
        p = self.p
        wname = {0: "l0_wout", 2: "l2_wout", 3: "l3_wout"}[layer]
        nkt = 16 if layer == 3 else 8
        self.WOB = self.sbuf("at_wo", (128, nkt * 1024), BF16, st)
        self.wload(self.WOB, wname, nkt * 1024)
        self.wo_nkt = nkt
        self.BOUTC = None
        if layer == 0:
            self.BOUTC = self.sbuf("at_bout", (128, 8), F32, st)
            p.dma("sync", self.BOUTC[:], self.dram["l0_bout"])
        self.PO = [PS_[0], PS_[1]]
        self.SOUT = [self.sbuf(f"at_so{i}", (128, 512), F32, st) for i in range(2)]
        self.AVNC = [self.sbuf(f"at_avnc{i}", (128, nkt * 512), BF16, st) for i in range(2)]
        self._po_free = [None, None]
        self._so_free = [None, None]
        self._avnc_free = [None, None]

    def _attn_outproj(self, layer, bb, tavn, bank_free):
        p = self.p
        nkt = self.wo_nkt
        WOB, PO, SOUT, AVNC, BOUTC = self.WOB, self.PO, self.SOUT, self.AVNC, self.BOUTC
        tlast = None
        po_free = [bank_free[0], bank_free[1]]
        so_free = self._so_free
        avnc_free = self._avnc_free
        pending = None
        def emit_out(mt, qc, pj, safe):
            nonlocal tlast
            if layer == 0:
                tev = p.op("vector", lambda e, mt=mt, pj=pj, S=SOUT, P=PO, B=BOUTC: e.tensor_scalar(
                    S[pj][:], P[pj][:], B[:, mt:mt + 1], None, ALU.add),
                    waits=[safe, so_free[pj]], inc=True)
            else:
                tev = p.op("vector", lambda e, pj=pj, S=SOUT, P=PO: e.tensor_copy(
                    S[pj][:], P[pj][:]),
                    waits=[safe, so_free[pj]], inc=True)
            po_free[pj] = tev
            tst = p.dma("sync",
                        self.dram["smix"].rearrange("p (k t) -> p k t", k=KT)[
                            :, mt, bb * T + qc * 512: bb * T + (qc + 1) * 512],
                        SOUT[pj][:], waits=[tev])
            so_free[pj] = tst
            tlast = tst
        for qc in range(4):
            jc = qc % 2
            tlv0 = p.dma("sync", AVNC[jc][:],
                         self.dram["avnt"].rearrange("p (h t) -> p h t", h=16)[
                             :, 0:nkt, qc * 512:(qc + 1) * 512],
                         waits=tavn + [avnc_free[jc]])
            tlv = self.slack(tlv0)
            for mt in range(8):
                pj = (qc * 8 + mt) % 2
                tmm = None
                for kt in range(nkt):
                    tmm = p.op("tensor", lambda e, mt=mt, qc=qc, kt=kt, pj=pj, jc=jc,
                               P=PO, W=WOB, V=AVNC: e.matmul(
                        P[pj][:], W[:, kt * 1024 + mt * 128: kt * 1024 + (mt + 1) * 128],
                        V[jc][:, kt * 512:(kt + 1) * 512],
                        start=(kt == 0), stop=(kt == nkt - 1)),
                        waits=([tlv] if mt == 0 and kt == 0 else []) +
                              ([po_free[pj]] if kt == 0 else []),
                        inc=(kt == nkt - 1), mode=(128, 128, False))
                if pending is not None:
                    emit_out(pending[0], pending[1], pending[2], tmm)
                pending = (mt, qc, pj)
            avnc_free[jc] = tmm
        tdr_o = p.op("tensor", lambda e: e.drain(), inc=True)
        emit_out(pending[0], pending[1], pending[2], tdr_o)
        self._avnc_free = avnc_free
        bank_free[0] = po_free[0]
        bank_free[1] = po_free[1]
        self._so_free = so_free
        return tlast

    def ph_mlp_4096(self, w1name, b1name, w2name, b2name, act1, act2,
                    s1name=None, s2name=None, dst="gffn", src="xnt", tcs=None):
        p = self.p
        st = contextlib.ExitStack()
        W1B = self.sbuf("mlp_w1", (128, KT * 4096), BF16, st)
        W2B = self.sbuf("mlp_w2", (128, 32 * 1024), BF16, st)
        tw1 = self.wload(W1B, w1name, KT * 4096)
        tw2 = self.wload(W2B, w2name, 32 * 1024)
        B1 = B2 = S1 = S2 = None
        tb = []
        if b1name:
            B1 = self.sbuf("mlp_b1", (128, 32), F32, st)
            tb.append(p.dma("sync", B1[:], self.dram[b1name]))
        if b2name:
            B2 = self.sbuf("mlp_b2", (128, 8), F32, st)
            tb.append(p.dma("sync", B2[:], self.dram[b2name]))
        if s1name:
            S1 = self.sbuf("mlp_s1", (128, 32), F32, st)
            tb.append(p.dma("sync", S1[:], self.dram[s1name]))
        if s2name:
            S2 = self.sbuf("mlp_s2", (128, 8), F32, st)
            tb.append(p.dma("sync", S2[:], self.dram[s2name]))
        p.barrier()
        XCH = [self.sbuf(f"mlp_x{i}", (128, KT * 512), BF16, st) for i in range(2)]
        HB = self.sbuf("mlp_h", (128, 32 * 512), BF16, st)
        OB = [self.sbuf(f"mlp_o{i}", (128, 512), self.dram[dst].dtype, st) for i in range(2)]
        PM = [self.psum(f"mlp_p{i}", (128, 512), F32, st) for i in range(2)]
        dep = {}
        for tc in (tcs if tcs is not None else range(NT // 512)):
            j = tc % 2
            d = dep.get(j, {})
            tl = self.slack(p.dma("sync", XCH[j][:], self.achunk(src, tc * 512, 512),
                                  waits=[d.get("x")]))
            pm_free = dep.get("pm", [None, None])
            th = [None] * 32
            def emit_h(mt, pj, safe):
                kw = {}
                if B1 is not None:
                    kw["bias"] = B1[:, mt:mt + 1]
                if S1 is not None:
                    kw["scale"] = S1[:, mt:mt + 1]
                th[mt] = p.op("scalar", lambda e, mt=mt, pj=pj, kw=kw: e.activation(
                    HB[:, mt * 512:(mt + 1) * 512], PM[pj][:], act1, **kw),
                    waits=[safe], inc=True)
                pm_free[pj] = th[mt]
            pend1 = None
            for mt in range(32):
                pj = mt % 2
                tmm = None
                for kt in range(KT):
                    tmm = p.op("tensor", lambda e, j=j, mt=mt, kt=kt, pj=pj: e.matmul(
                        PM[pj][:], W1B[:, kt * 4096 + mt * 128: kt * 4096 + (mt + 1) * 128],
                        XCH[j][:, kt * 512:(kt + 1) * 512],
                        start=(kt == 0), stop=(kt == KT - 1)),
                        waits=[tl if mt == 0 and kt == 0 else None,
                               dep.get("hread") if mt == 0 and kt == 0 else None,
                               pm_free[pj] if kt == 0 else None],
                        inc=(kt == KT - 1), mode=(128, 128, False))
                if pend1 is not None:
                    emit_h(pend1[0], pend1[1], tmm)
                pend1 = (mt, pj)
            tdr1 = p.op("tensor", lambda e: e.drain(), inc=True)
            emit_h(pend1[0], pend1[1], tdr1)
            tg2 = None
            ob_free = dep.get("ob", [None, None])
            def emit_o(mt, pj, safe):
                kw = {}
                if B2 is not None:
                    kw["bias"] = B2[:, mt:mt + 1]
                if S2 is not None:
                    kw["scale"] = S2[:, mt:mt + 1]
                if act2 is None:
                    tev = p.op("vector", lambda e, mt=mt, pj=pj: e.tensor_scalar(
                        OB[pj][:], PM[pj][:], B2[:, mt:mt + 1] if B2 is not None else 0.0,
                        None, ALU.add),
                        waits=[safe, ob_free[pj]], inc=True)
                else:
                    tev = p.op("scalar", lambda e, mt=mt, pj=pj, kw=kw, jo=pj: e.activation(
                        OB[jo][:], PM[pj][:], act2, **kw),
                        waits=[safe, ob_free[pj]], inc=True)
                pm_free[pj] = tev
                tso = p.dma("scalar", self.achunk(dst, tc * 512, 512)[:, mt, :], OB[pj][:],
                            waits=[tev])
                ob_free[pj] = tso
            pend2 = None
            for mt in range(8):
                pj = mt % 2
                tmm = None
                for kt in range(32):
                    tmm = p.op("tensor", lambda e, mt=mt, kt=kt, pj=pj: e.matmul(
                        PM[pj][:], W2B[:, kt * 1024 + mt * 128: kt * 1024 + (mt + 1) * 128],
                        HB[:, kt * 512:(kt + 1) * 512],
                        start=(kt == 0), stop=(kt == 31)),
                        waits=[th[31] if mt == 0 and kt == 0 else None,
                               th[kt] if mt == 0 else None,
                               pm_free[pj] if kt == 0 else None],
                        inc=(kt == 31), mode=(128, 128, False))
                if pend2 is not None:
                    emit_o(pend2[0], pend2[1], tmm)
                pend2 = (mt, pj)
                tg2 = tmm
            tdr2 = p.op("tensor", lambda e: e.drain(), inc=True)
            emit_o(pend2[0], pend2[1], tdr2)
            dep[j] = {"x": th[31]}
            dep["pm"] = pm_free
            dep["ob"] = ob_free
            dep["hread"] = tg2
        p.barrier()
        st.close()

    def ph_qkv_fm(self, layer):
        p = self.p
        st = contextlib.ExitStack()
        wname = "l2_wqkv" if layer == 2 else "l3_wqkv"
        M = 3072 if layer == 2 else 4096
        nq = 8 if layer == 2 else 16
        WQ = self.sbuf("qf_w", (128, KT * M), BF16, st)
        twq = self.wload(WQ, wname, KT * M)
        XCH = [self.sbuf(f"qf_x{i}", (128, KT * 512), BF16, st) for i in range(2)]
        QS = [self.sbuf(f"qf_qs{i}", (128, 512), BF16, st) for i in range(2)]
        PM = [self.psum(f"qf_p{i}", (128, 512), F32, st) for i in range(2)]
        dep = {}
        voff = (2048 if layer == 2 else 3072)
        for tc in range(NT // 512):
            j = tc % 2
            d = dep.get(j, {})
            b_, tr0 = tc // 4, (tc % 4) * 512
            tl0 = p.dma("sync", XCH[j][:], self.achunk("xnt", tc * 512, 512), waits=[d.get("x")])
            tl = self.slack(tl0)
            pm_free = dep.get("pm", [None, None])
            qs_free = dep.get("qs", [None, None])

            def emit_qk(ft, pj, safe, b_=b_, tr0=tr0):
                tev = p.op("scalar", lambda e, pj=pj: e.copy(QS[pj][:], PM[pj][:]),
                           waits=[safe, qs_free[pj]], inc=True)
                pm_free[pj] = tev
                if ft < nq:
                    dst = self.dram["qt"].rearrange("p (b h t) -> p b h t", b=BC, h=16)[
                        :, b_, ft, tr0:tr0 + 512]
                else:
                    dst = self.dram["kt_"].rearrange("p (b h t) -> p b h t", b=BC, h=8)[
                        :, b_, ft - nq, tr0:tr0 + 512]
                tst = p.dma("scalar", dst, QS[pj][:], waits=[tev])
                qs_free[pj] = tst

            def emit_v(mtt, vc, pj, safe, b_=b_, tc=tc):
                tev = p.op("scalar", lambda e, pj=pj: e.copy(QS[pj][:], PM[pj][:]),
                           waits=[safe, qs_free[pj]], inc=True)
                pm_free[pj] = tev
                gtt = tc * 4 + mtt
                dst = self.dram["vt"][:, gtt * 1024 + vc * 512: gtt * 1024 + (vc + 1) * 512]
                tst = p.dma("scalar", dst, QS[pj][:], waits=[tev])
                qs_free[pj] = tst

            pend = None
            vpend = None
            ngrp = 0
            for ft in range(nq + 8):
                pj = ngrp % 2
                tmm = None
                for kt in range(KT):
                    tmm = p.op("tensor", lambda e, ft=ft, kt=kt, pj=pj, j=j: e.matmul(
                        PM[pj][:], WQ[:, kt * M + ft * 128: kt * M + (ft + 1) * 128],
                        XCH[j][:, kt * 512:(kt + 1) * 512],
                        start=(kt == 0), stop=(kt == KT - 1)),
                        waits=[tl if ngrp == 0 and kt == 0 else None,
                               twq if tc == 0 and ngrp == 0 and kt == 0 else None,
                               pm_free[pj] if kt == 0 else None],
                        inc=(kt == KT - 1), mode=(128, 128, False))
                if pend is not None:
                    emit_qk(pend[0], pend[1], tmm)
                pend = (ft, pj)
                ngrp += 1
            for mtt in range(4):
                for vc in range(2):
                    pj = ngrp % 2
                    tmm = None
                    for kt in range(KT):
                        tmm = p.op("tensor", lambda e, mtt=mtt, vc=vc, kt=kt, pj=pj, j=j: e.matmul(
                            PM[pj][:], XCH[j][:, kt * 512 + mtt * 128: kt * 512 + (mtt + 1) * 128],
                            WQ[:, kt * M + voff + vc * 512: kt * M + voff + (vc + 1) * 512],
                            start=(kt == 0), stop=(kt == KT - 1)),
                            waits=[pm_free[pj] if kt == 0 else None],
                            inc=(kt == KT - 1), mode=(128, 128, False))
                    if vpend is not None:
                        emit_v(vpend[0], vpend[1], vpend[2], tmm)
                    elif pend is not None:
                        emit_qk(pend[0], pend[1], tmm)
                        pend = None
                    vpend = (mtt, vc, pj)
                    ngrp += 1
            tdr = p.op("tensor", lambda e: e.drain(), inc=True)
            emit_v(vpend[0], vpend[1], vpend[2], tdr)
            dep[j] = {"x": tdr}
            dep["pm"] = pm_free
            dep["qs"] = qs_free
        p.barrier()
        st.close()

    def ph_gemm(self, src, dst, wname, nkt, nmt, act, biasname=None, scalename=None,
                mt0=0, mt1=None, residual=None, act2=None):
        p = self.p
        st = contextlib.ExitStack()
        mt1 = mt1 if mt1 is not None else nmt
        nmtl = mt1 - mt0
        Mfull = nmt * 128
        WB = self.sbuf("gg_w", (128, nkt * nmtl * 128), BF16, st)
        tws = None
        for kt in range(nkt):
            tws = p.dma("gpsimd", WB[:, kt * nmtl * 128:(kt + 1) * nmtl * 128],
                        self.dram[wname][:, kt * Mfull + mt0 * 128: kt * Mfull + mt1 * 128])
        tw = self.slack(tws)
        COLS = {}
        for nm, name in (("b", biasname), ("s", scalename)):
            if name:
                COLS[nm] = self.sbuf("gg_" + nm, (128, nmt), F32, st)
                p.dma("sync", COLS[nm][:], self.dram[name])
        p.barrier()
        XCH = [self.sbuf(f"gg_x{i}", (128, nkt * 512), BF16, st) for i in range(2)]
        ob_dt = self.dram[dst].dtype
        OB = [self.sbuf(f"gg_o{i}", (128, 512), ob_dt, st) for i in range(2)]
        PM = [self.psum(f"gg_p{i}", (128, 512), F32, st) for i in range(2)]
        dep = {}
        for tc in range(NT // 512):
            j = tc % 2
            d = dep.get(j, {})
            tl0 = p.dma("sync", XCH[j][:],
                        self.dram[src].rearrange("p (k t) -> p k t", k=nkt)[:, :, tc * 512:(tc + 1) * 512],
                        waits=[d.get("x")])
            tl = self.slack(tl0)
            pm_free = dep.get("pm", [None, None])
            ob_free = dep.get("ob", [None, None])

            def emit(mt, pj, safe, tc=tc):
                kw = {}
                if "b" in COLS:
                    kw["bias"] = COLS["b"][:, mt:mt + 1]
                if "s" in COLS:
                    kw["scale"] = COLS["s"][:, mt:mt + 1]
                tev = p.op("scalar", lambda e, pj=pj, kw=kw: e.activation(
                    OB[pj][:], PM[pj][:], act, **kw),
                    waits=[safe, ob_free[pj]], inc=True)
                pm_free[pj] = tev
                if act2 is not None:
                    tev = p.op("scalar", lambda e, pj=pj: e.activation(
                        OB[pj][:], OB[pj][:], act2), waits=[tev], inc=True)
                tst = p.dma("scalar",
                            self.dram[dst].rearrange("p (k t) -> p k t", k=nmt)[
                                :, mt, tc * 512:(tc + 1) * 512],
                            OB[pj][:], waits=[tev])
                ob_free[pj] = tst

            pend = None
            tmm = None
            for mtl in range(nmtl):
                mt = mt0 + mtl
                pj = mtl % 2
                for kt in range(nkt):
                    tmm = p.op("tensor", lambda e, mtl=mtl, kt=kt, pj=pj, j=j: e.matmul(
                        PM[pj][:], WB[:, kt * nmtl * 128 + mtl * 128: kt * nmtl * 128 + (mtl + 1) * 128],
                        XCH[j][:, kt * 512:(kt + 1) * 512],
                        start=(kt == 0), stop=(kt == nkt - 1)),
                        waits=[tl if mtl == 0 and kt == 0 else None,
                               tw if tc == 0 and mtl == 0 and kt == 0 else None,
                               pm_free[pj] if kt == 0 else None],
                        inc=(kt == nkt - 1), mode=(128, 128, False))
                if pend is not None:
                    emit(pend[0], pend[1], tmm)
                pend = (mt, pj)
            tdr = p.op("tensor", lambda e: e.drain(), inc=True)
            emit(pend[0], pend[1], tdr)
            dep[j] = {"x": tmm}
            dep["pm"] = pm_free
            dep["ob"] = ob_free
        p.barrier()
        st.close()

    def ph_l3_gate(self):
        p = self.p
        st = contextlib.ExitStack()
        W1B = self.sbuf("lg_w1", (128, KT * 4096), BF16, st)
        W2B = self.sbuf("lg_w2", (128, KT * 4096), BF16, st)
        tw1 = self.wload(W1B, "l3_w1", KT * 4096)
        tw2 = self.wload(W2B, "l3_w2", KT * 4096)
        S1 = self.sbuf("lg_s1", (128, 32), F32, st)
        S2 = self.sbuf("lg_s2", (128, 32), F32, st)
        p.dma("sync", S1[:], self.dram["l3_s1"])
        p.dma("sync", S2[:], self.dram["l3_s2"])
        p.barrier()
        XCH = [self.sbuf(f"lg_x{i}", (128, KT * 512), BF16, st) for i in range(2)]
        TA = [self.sbuf(f"lg_ta{i}", (128, 512), BF16, st) for i in range(2)]
        TB = [self.sbuf(f"lg_tb{i}", (128, 512), BF16, st) for i in range(2)]
        OB = [self.sbuf(f"lg_o{i}", (128, 512), BF16, st) for i in range(2)]
        PM = [self.psum(f"lg_p{i}", (128, 512), F32, st) for i in range(4)]
        dep = {}
        for tc in range(NT // 512):
            j = tc % 2
            d = dep.get(j, {})
            tl0 = p.dma("sync", XCH[j][:], self.achunk("xnt", tc * 512, 512), waits=[d.get("x")])
            tl = self.slack(tl0)
            pm_free = dep.get("pm", [None] * 4)
            buf_free = dep.get("bf", [None, None])

            def emit(mt, pj, safe, tc=tc):
                t1 = p.op("scalar", lambda e, mt=mt, pj=pj: e.activation(
                    TA[pj][:], PM[pj][:], AF.Silu, scale=S1[:, mt:mt + 1]),
                    waits=[safe], inc=True)
                t2 = p.op("vector", lambda e, mt=mt, pj=pj: e.tensor_scalar(
                    TB[pj][:], PM[pj + 2][:], S2[:, mt:mt + 1], None, ALU.mult),
                    waits=[safe], inc=True)
                t3 = p.op("vector", lambda e, pj=pj: e.tensor_tensor(
                    OB[pj][:], TA[pj][:], TB[pj][:], ALU.mult),
                    waits=[t1, t2, buf_free[pj]], inc=True)
                pm_free[pj] = t3
                pm_free[pj + 2] = t3
                tst = p.dma("scalar",
                            self.dram["hbig1"].rearrange("p (k t) -> p k t", k=32)[
                                :, mt, tc * 512:(tc + 1) * 512],
                            OB[pj][:], waits=[t3])
                buf_free[pj] = tst

            pend = None
            for mt in range(32):
                pj = mt % 2
                tmm = None
                for kt in range(KT):
                    p.op("tensor", lambda e, mt=mt, kt=kt, pj=pj, j=j: e.matmul(
                        PM[pj][:], W1B[:, kt * 4096 + mt * 128: kt * 4096 + (mt + 1) * 128],
                        XCH[j][:, kt * 512:(kt + 1) * 512],
                        start=(kt == 0), stop=(kt == KT - 1)),
                        waits=[tl if mt == 0 and kt == 0 else None,
                               tw1 if tc == 0 and mt == 0 and kt == 0 else None,
                               pm_free[pj] if kt == 0 else None],
                        mode=(128, 128, False))
                for kt in range(KT):
                    tmm = p.op("tensor", lambda e, mt=mt, kt=kt, pj=pj, j=j: e.matmul(
                        PM[pj + 2][:], W2B[:, kt * 4096 + mt * 128: kt * 4096 + (mt + 1) * 128],
                        XCH[j][:, kt * 512:(kt + 1) * 512],
                        start=(kt == 0), stop=(kt == KT - 1)),
                        waits=[tw2 if tc == 0 and mt == 0 and kt == 0 else None,
                               pm_free[pj + 2] if kt == 0 else None],
                        inc=(kt == KT - 1), mode=(128, 128, False))
                if pend is not None:
                    emit(pend[0], pend[1], tmm)
                pend = (mt, pj)
            tdr = p.op("tensor", lambda e: e.drain(), inc=True)
            emit(pend[0], pend[1], tdr)
            dep[j] = {"x": tdr}
            dep["pm"] = pm_free
            dep["bf"] = buf_free
        p.barrier()
        st.close()

    def ph_l2_gate(self):
        p = self.p
        st = contextlib.ExitStack()
        WG = self.sbuf("g2_wg", (128, KT * 8), BF16, st)
        twg = self.wload(WG, "l2_wg", KT * 8)
        BGC = self.sbuf("g2_bg", (8, 1), F32, st)
        tbg = p.dma("sync", BGC[:], self.dram["l2_bg_col"])
        XCH = [self.sbuf(f"g2_x{i}", (128, KT * 512), BF16, st) for i in range(2)]
        LG = self.sbuf("g2_lg", (8, NT), F32, st)
        PM = [self.psum(f"g2_p{i}", (8, 512), F32, st) for i in range(2)]
        p.barrier()
        dep = {}
        pend = None
        pm_free = [None, None]
        tlgs = []

        def emitg(tc, pj, safe):
            tev = p.op("vector", lambda e, tc=tc, pj=pj: e.tensor_scalar(
                LG[:, tc * 512:(tc + 1) * 512], PM[pj][:], BGC[:], None, ALU.add),
                waits=[safe, tbg], inc=True)
            pm_free[pj] = tev
            tlgs.append(tev)

        for tc in range(NT // 512):
            j = tc % 2
            d = dep.get(j, {})
            tl0 = p.dma("sync", XCH[j][:], self.achunk("xnt", tc * 512, 512), waits=[d.get("x")])
            tl = self.slack(tl0)
            pj = tc % 2
            tmm = None
            for kt in range(KT):
                tmm = p.op("tensor", lambda e, kt=kt, pj=pj, j=j: e.matmul(
                    PM[pj][:], WG[:, kt * 8:(kt + 1) * 8], XCH[j][:, kt * 512:(kt + 1) * 512],
                    start=(kt == 0), stop=(kt == KT - 1)),
                    waits=[tl if kt == 0 else None,
                           twg if tc == 0 and kt == 0 else None,
                           pm_free[pj] if kt == 0 else None],
                    inc=(kt == KT - 1), mode=(128, 32, False))
            if pend is not None:
                emitg(pend[0], pend[1], tmm)
            pend = (tc, pj)
            dep[j] = {"x": tmm}
        tdr = p.op("tensor", lambda e: e.drain(), inc=True)
        emitg(pend[0], pend[1], tdr)
        LGT = self.sbuf("g2_lgt", (128, 32 * 8), F32, st)
        PT = [self.psum(f"g2_pt{i}", (128, 128), F32, st) for i in range(2)]
        tevs = [None, None]
        for tt in range(32):
            pj = tt % 2
            p.op("tensor", lambda e, tt=tt, pj=pj: e.transpose(
                PT[pj][0:128, 0:8], LG[:, tt * 128:(tt + 1) * 128], self.c_ident_f[0:8, 0:8]),
                waits=[self.slack(tlgs[-1], 1) if tt == 0 else None, tevs[pj]], mode=(32, 128, True))
            ttr = p.op("tensor", lambda e: e.drain(), inc=True)
            tevs[pj] = p.op("scalar", lambda e, tt=tt, pj=pj: e.copy(
                LGT[:, tt * 8:(tt + 1) * 8], PT[pj][0:128, 0:8]), waits=[ttr], inc=True)
        tprep = tevs[1]
        W_ = lambda t: t.rearrange("p (g n) -> p g n", n=8)
        EX = self.sbuf("g2_ex", (128, 256), F32, st)
        R1 = self.sbuf("g2_r1", (128, 32 * 6), F32, st)
        SELW = self.sbuf("g2_selw", (128, 256), F32, st)
        m1 = R1[:, 0:32]; m2 = R1[:, 32:64]; S_ = R1[:, 64:96]
        Es = R1[:, 96:128]; den = R1[:, 128:160]; rec = R1[:, 160:192]
        t0 = p.op("scalar", lambda e: e.activation(EX[:], LGT[:], AF.Exp),
                  waits=[tevs[0], tevs[1]], inc=True)
        t1 = p.op("vector", lambda e: e.tensor_reduce(m1, W_(LGT[:]), mybir.AxisListType.X, ALU.max),
                  waits=[tevs[0], tevs[1]], inc=True)
        t2 = p.op("vector", lambda e: e.tensor_tensor(
            W_(SELW[:]), W_(LGT[:]), m1.unsqueeze(2).broadcast_to([128, 32, 8]), ALU.is_equal),
            waits=[t1], inc=True)
        t3 = p.op("vector", lambda e: e.scalar_tensor_tensor(
            W_(SELW[:]), W_(SELW[:]), -1e30, W_(LGT[:]), ALU.mult, ALU.add),
            waits=[t2], inc=True)
        t4 = p.op("vector", lambda e: e.tensor_reduce(m2, W_(SELW[:]), mybir.AxisListType.X, ALU.max),
                  waits=[t3], inc=True)
        t5 = p.op("vector", lambda e: e.tensor_tensor(
            W_(SELW[:]), W_(LGT[:]), m2.unsqueeze(2).broadcast_to([128, 32, 8]), ALU.is_ge),
            waits=[t4], inc=True)
        t6 = p.op("vector", lambda e: e.tensor_tensor(W_(SELW[:]), W_(SELW[:]), W_(EX[:]), ALU.mult),
                  waits=[t5, t0], inc=True)
        t7 = p.op("vector", lambda e: e.tensor_reduce(S_, W_(SELW[:]), mybir.AxisListType.X, ALU.add),
                  waits=[t6], inc=True)
        t8 = p.op("vector", lambda e: e.tensor_reduce(Es, W_(EX[:]), mybir.AxisListType.X, ALU.add),
                  waits=[t0, t6], inc=True)
        t9 = p.op("vector", lambda e: e.scalar_tensor_tensor(den, Es, 1e-8, S_, ALU.mult, ALU.add),
                  waits=[t7, t8], inc=True)
        t10 = p.op("vector", lambda e: e.reciprocal(rec, den), waits=[t9], inc=True)
        t11 = p.op("vector", lambda e: e.tensor_tensor(
            W_(SELW[:]), W_(SELW[:]), rec.unsqueeze(2).broadcast_to([128, 32, 8]), ALU.mult),
            waits=[t10], inc=True)
        SELTB = self.sbuf("g2_selt", (8, NT), F32, st)
        tevs2 = [None, None]
        for tt in range(32):
            pj = tt % 2
            p.op("tensor", lambda e, tt=tt, pj=pj: e.transpose(
                PT[pj][0:8, 0:128], SELW[:, tt * 8:(tt + 1) * 8], self.c_ident_f[:]),
                waits=[self.slack(t11, 1) if tt == 0 else (t11 if tt == 1 else None),
                       tevs2[pj]], mode=(128, 32, True))
            ttr = p.op("tensor", lambda e: e.drain(), inc=True)
            tevs2[pj] = p.op("scalar", lambda e, tt=tt, pj=pj: e.copy(
                SELTB[:, tt * 128:(tt + 1) * 128], PT[pj][0:8, 0:128]), waits=[ttr], inc=True)
        tst = p.dma("scalar", self.dram["selt"], SELTB[:], waits=[tevs2[0], tevs2[1]])
        p.barrier()
        st.close()

    def ph_l2_ffn(self):
        CW = 256
        p = self.p
        st = contextlib.ExitStack()
        W1B = self.sbuf("e2_w1", (128, KT * 4096), BF16, st)
        W2B = self.sbuf("e2_w2", (128, 32 * 1024), BF16, st)
        B1 = self.sbuf("e2_b1", (128, 32), F32, st)
        B2 = self.sbuf("e2_b2", (128, 8), F32, st)
        SROW = self.sbuf("e2_srow", (1, NT), F32, st)
        SREP = self.sbuf("e2_srep", (128, NT), BF16, st)
        XCH = [self.sbuf(f"e2_x{i}", (128, KT * CW), BF16, st) for i in range(2)]
        ACC = [self.sbuf(f"e2_a{i}", (128, KT * CW), F32, st) for i in range(2)]
        HB = self.sbuf("e2_h", (128, 32 * CW), BF16, st)
        TBF = [self.sbuf(f"e2_t{i}", (128, 512), F32, st) for i in range(2)]
        OB = [self.sbuf(f"e2_o{i}", (128, 512), F32, st) for i in range(2)]
        PM = [self.psum(f"e2_p{i}", (128, 512), F32, st) for i in range(2)]
        PR = self.psum("e2_pr", (128, 512), F32, st)
        dep = {}
        for e in range(8):
            tw1 = self.wload(W1B, f"l2_w1_{e}", KT * 4096, after=dep.get("wfree"))
            tw2 = self.wload(W2B, f"l2_w2_{e}", 32 * 1024)
            tb1 = p.dma("sync", B1[:], self.dram[f"l2_b1_{e}"], waits=[dep.get("bfree")])
            tb2 = p.dma("sync", B2[:], self.dram[f"l2_b2_{e}"], waits=[dep.get("bfree")])
            tsr = p.dma("sync", SROW[:], self.dram["selt"][e:e + 1, :], waits=[dep.get("srfree")])
            tprev = dep.get("srepfree")
            trep0 = None
            for qc in range(NT // 512):
                p.op("tensor", lambda e_, qc=qc: e_.matmul(
                    PR[:], self.c_ones65[0:1, :], SROW[:, qc * 512:(qc + 1) * 512],
                    start=True, stop=True),
                    waits=[self.slack(tsr, 1) if qc == 0 else None, tprev], mode=(32, 128, False))
                tdr = p.op("tensor", lambda e_: e_.drain(), inc=True)
                tprev = p.op("scalar", lambda e_, qc=qc: e_.copy(
                    SREP[:, qc * 512:(qc + 1) * 512], PR[:]), waits=[tdr], inc=True)
                if trep0 is None:
                    trep0 = tprev
            tsel = tprev
            pm_free = dep.get("pm", [None, None])
            ob_free = dep.get("ob", [None, None])
            tb_free = dep.get("tb", [None, None])
            hread = dep.get("hread")
            g2last = None
            for tc in range(NT // CW):
                j = tc % 2
                tl0 = p.dma("sync", XCH[j][:], self.achunk("xnt", tc * CW, CW),
                            waits=[dep.get((j, "x"))])
                tl = self.slack(tl0)
                tacc = None
                if e > 0:
                    tacc = p.dma("gpsimd", ACC[j][:], self.achunk("acct", tc * CW, CW),
                                 waits=[dep.get((j, "acc"))])
                th = [None] * 32

                def emit_h(mt, pj, safe):
                    th[mt] = p.op("scalar", lambda e_, mt=mt, pj=pj: e_.activation(
                        HB[:, mt * CW:(mt + 1) * CW], PM[pj][:, 0:CW], AF.Gelu_apprx_tanh,
                        bias=B1[:, mt:mt + 1]),
                        waits=[safe, tb1 if mt == 0 else None], inc=True)
                    pm_free[pj] = th[mt]

                pend = None
                for mt in range(32):
                    pj = mt % 2
                    tmm = None
                    for kt in range(KT):
                        tmm = p.op("tensor", lambda e_, mt=mt, kt=kt, pj=pj, j=j: e_.matmul(
                            PM[pj][:, 0:CW], W1B[:, kt * 4096 + mt * 128: kt * 4096 + (mt + 1) * 128],
                            XCH[j][:, kt * CW:(kt + 1) * CW],
                            start=(kt == 0), stop=(kt == KT - 1)),
                            waits=[tl if mt == 0 and kt == 0 else None,
                                   tw1 if tc == 0 and mt == 0 and kt == 0 else None,
                                   hread if tc == 0 and mt == 0 and kt == 0 else None,
                                   pm_free[pj] if kt == 0 else None],
                            inc=(kt == KT - 1), mode=(128, 128, False))
                    if pend is not None:
                        emit_h(pend[0], pend[1], tmm)
                    pend = (mt, pj)
                tdr1 = p.op("tensor", lambda e_: e_.drain(), inc=True)
                emit_h(pend[0], pend[1], tdr1)

                def emit_o(mt, pj, safe, tc=tc, j=j, e=e, tacc=tacc):
                    tt1 = p.op("vector", lambda e_, mt=mt, pj=pj, tc=tc: e_.scalar_tensor_tensor(
                        TBF[pj][:, 0:CW], PM[pj][:, 0:CW], B2[:, mt:mt + 1],
                        SREP[:, tc * CW:(tc + 1) * CW], ALU.add, ALU.mult),
                        waits=[safe, tsel if mt == 0 else None,
                               tb2 if mt == 0 else None, tb_free[pj]], inc=True)
                    pm_free[pj] = tt1
                    if e > 0:
                        tt2 = p.op("vector", lambda e_, mt=mt, pj=pj, j=j: e_.tensor_tensor(
                            OB[pj][:, 0:CW], TBF[pj][:, 0:CW],
                            ACC[j][:, mt * CW:(mt + 1) * CW], ALU.add),
                            waits=[tt1, tacc, ob_free[pj]], inc=True)
                    else:
                        tt2 = p.op("vector", lambda e_, pj=pj: e_.tensor_copy(
                            OB[pj][:, 0:CW], TBF[pj][:, 0:CW]), waits=[tt1, ob_free[pj]], inc=True)
                    tb_free[pj] = tt2
                    tst = p.dma("sync", self.achunk("acct", tc * CW, CW)[:, mt, :],
                                OB[pj][:, 0:CW], waits=[tt2])
                    ob_free[pj] = tst
                    return tst

                pend2 = None
                tmm = None
                tstl = None
                for mt in range(8):
                    pj = mt % 2
                    for kt in range(32):
                        tmm = p.op("tensor", lambda e_, mt=mt, kt=kt, pj=pj: e_.matmul(
                            PM[pj][:, 0:CW], W2B[:, kt * 1024 + mt * 128: kt * 1024 + (mt + 1) * 128],
                            HB[:, kt * CW:(kt + 1) * CW],
                            start=(kt == 0), stop=(kt == 31)),
                            waits=[th[31] if mt == 0 and kt == 0 else None,
                                   th[kt] if mt == 0 else None,
                                   tw2 if tc == 0 and mt == 0 and kt == 0 else None,
                                   pm_free[pj] if kt == 0 else None],
                            inc=(kt == 31), mode=(128, 128, False))
                    if pend2 is not None:
                        tstl = emit_o(pend2[0], pend2[1], tmm)
                    pend2 = (mt, pj)
                    g2last = tmm
                tdr2 = p.op("tensor", lambda e_: e_.drain(), inc=True)
                tstl = emit_o(pend2[0], pend2[1], tdr2)
                dep[(j, "x")] = th[31]
                dep[(j, "acc")] = tstl
            dep["pm"] = pm_free
            dep["ob"] = ob_free
            dep["tb"] = tb_free
            dep["hread"] = g2last
            dep["wfree"] = g2last
            dep["bfree"] = g2last
            dep["srfree"] = trep0
            dep["srepfree"] = g2last
        p.barrier()
        dep2 = {}
        for tc in range(NT // CW):
            j = tc % 2
            d = dep2.get(j, {})
            tl = p.dma("sync", ACC[j][:], self.achunk("acct", tc * CW, CW), waits=[d.get("a")])
            tso = p.op("scalar", lambda e_, j=j: e_.activation(ACC[j][:], ACC[j][:], AF.Silu),
                       waits=[tl, d.get("o")], inc=True)
            tst = p.dma("sync", self.achunk("gffn", tc * CW, CW), ACC[j][:], waits=[tso])
            dep2[j] = {"a": tst, "o": tst}
        p.barrier()
        st.close()

    def ph_l1_rms(self):
        p = self.p
        st = contextlib.ExitStack()
        W = 512
        NCH = NT // W
        NW = self.sbuf("rm_nw", (128, 8), F32, st)
        p.dma("sync", NW[:], self.dram["l1_norm_w"])
        XCH = [self.sbuf(f"rm_x{i}", (128, KT * W), F32, st) for i in range(2)]
        XSQ = [self.sbuf(f"rm_xsq{i}", (128, KT * W), BF16, st) for i in range(2)]
        ROWS = [self.sbuf(f"rm_rows{i}", (1, W * 2), F32, st) for i in range(2)]
        RSB = [self.sbuf(f"rm_rsb{i}", (128, W), F32, st) for i in range(2)]
        XN = [self.sbuf(f"rm_xn{i}", (128, KT * W), F32, st) for i in range(2)]
        XNB = [self.sbuf(f"rm_xnb{i}", (128, KT * W), BF16, st) for i in range(2)]
        PQ = [self.psum(f"rm_pq{i}", (1, W), F32, st) for i in range(2)]
        PR1 = [self.psum(f"rm_pr1{i}", (128, W), F32, st) for i in range(2)]
        p.barrier()
        dep = {}
        for c in range(NCH):
            j = c % 2
            d = dep.get(j, {})
            tl = p.dma("sync", XCH[j][:], self.achunk("xnf", c * W, W), waits=[d.get("xch")])
            tsq = p.op("scalar", lambda e, j=j: e.activation(XSQ[j][:], XCH[j][:], AF.Square),
                       waits=[tl, d.get("xsq")], inc=True)
            tsq = self.slack(tsq, 1)
            tstat = None
            for kt in range(KT):
                tstat = p.op("tensor", lambda e, j=j, kt=kt: e.matmul(
                    PQ[j][:], self.c_ones[:], XSQ[j][:, kt * W:(kt + 1) * W],
                    start=(kt == 0), stop=(kt == KT - 1)),
                    waits=[tsq, d.get("pq")] if kt == 0 else (), mode=(128, 32, False))
            tstat = p.op("tensor", lambda e: e.drain(), inc=True)
            R = ROWS[j]
            qrow, rrow = R[:, 0:W], R[:, W:2 * W]
            tps = p.op("vector", lambda e, qrow=qrow, j=j: e.tensor_scalar(
                qrow, PQ[j][:], 1.0 / 1024.0, None, ALU.mult),
                waits=[tstat, d.get("rows")], inc=True)
            p.op("scalar", lambda e, qrow=qrow: e.activation(qrow, qrow, AF.Ln,
                 bias=self.c_misc[0:1, 0:1]), waits=[tps])
            trs = p.op("scalar", lambda e, rrow=rrow, qrow=qrow: e.activation(
                rrow, qrow, AF.Exp, scale=-0.5), inc=True)
            p.op("tensor", lambda e, j=j, rrow=rrow: e.matmul(
                PR1[j][:], self.c_ones65[0:1, :], rrow, start=True, stop=True),
                waits=[self.slack(trs), d.get("pr")], mode=(32, 128, False))
            trep = p.op("tensor", lambda e: e.drain(), inc=True)
            tevr = p.op("scalar", lambda e, j=j: e.copy(RSB[j][:], PR1[j][:]),
                        waits=[trep, d.get("rsb")], inc=True)
            bview = lambda t: t[:].unsqueeze(1).broadcast_to([128, KT, W])
            xview = lambda t: t[:].rearrange("p (k t) -> p k t", k=KT)
            tmul = p.op("vector", lambda e, j=j, bview=bview, xview=xview: e.tensor_tensor(
                xview(XN[j]), xview(XCH[j]), bview(RSB[j]), ALU.mult),
                waits=[tevr, d.get("xn")], inc=True)
            tnw = None
            for kt in range(KT):
                tnw = p.op("vector", lambda e, j=j, kt=kt: e.tensor_scalar(
                    XNB[j][:, kt * W:(kt + 1) * W], XN[j][:, kt * W:(kt + 1) * W],
                    NW[:, kt:kt + 1], None, ALU.mult),
                    waits=[tmul, d.get("xnb")] if kt == 0 else [tmul], inc=True)
            tsb = p.dma("sync", self.achunk("hmt", c * W, W), XNB[j][:], waits=[tnw])
            dep[j] = {"xch": tmul, "xsq": tstat, "pq": tps, "rows": trep,
                      "pr": tevr, "rsb": tmul, "xn": tnw, "xnb": tsb}
        p.barrier()
        st.close()

    def ph_l1_inproj(self):
        p = self.p
        st = contextlib.ExitStack()
        WB = self.sbuf("ip_w", (128, KT * 4096), BF16, st)
        tw = self.wload(WB, "l1_inproj", KT * 4096)
        p.barrier()
        XCH = [self.sbuf(f"ip_x{i}", (128, KT * 512), BF16, st) for i in range(2)]
        OU = [self.sbuf(f"ip_ou{i}", (128, 512), F32, st) for i in range(2)]
        OZ = [self.sbuf(f"ip_oz{i}", (128, 512), BF16, st) for i in range(2)]
        PM = [self.psum(f"ip_p{i}", (128, 512), F32, st) for i in range(2)]
        dep = {}
        for tc in range(NT // 512):
            j = tc % 2
            d = dep.get(j, {})
            tl0 = p.dma("sync", XCH[j][:], self.achunk("hmt", tc * 512, 512), waits=[d.get("x")])
            tl = self.slack(tl0)
            pm_free = dep.get("pm", [None, None])
            o_free = dep.get("o", [None, None])

            def emit(mt, pj, safe, tc=tc):
                if mt < 16:
                    tev = p.op("scalar", lambda e, pj=pj: e.copy(OU[pj][:], PM[pj][:]),
                               waits=[safe, o_free[pj]], inc=True)
                    dst = self.dram["ut"].rearrange("p (k t) -> p k t", k=16)[
                        :, mt, tc * 512:(tc + 1) * 512]
                    tst = p.dma("scalar", dst, OU[pj][:], waits=[tev])
                else:
                    tev = p.op("scalar", lambda e, pj=pj: e.copy(OZ[pj][:], PM[pj][:]),
                               waits=[safe, o_free[pj]], inc=True)
                    dst = self.dram["zt"].rearrange("p (k t) -> p k t", k=16)[
                        :, mt - 16, tc * 512:(tc + 1) * 512]
                    tst = p.dma("scalar", dst, OZ[pj][:], waits=[tev])
                pm_free[pj] = tev
                o_free[pj] = tst

            pend = None
            for mt in range(32):
                pj = mt % 2
                tmm = None
                for kt in range(KT):
                    tmm = p.op("tensor", lambda e, mt=mt, kt=kt, pj=pj, j=j: e.matmul(
                        PM[pj][:], WB[:, kt * 4096 + mt * 128: kt * 4096 + (mt + 1) * 128],
                        XCH[j][:, kt * 512:(kt + 1) * 512],
                        start=(kt == 0), stop=(kt == KT - 1)),
                        waits=[tl if mt == 0 and kt == 0 else None,
                               tw if tc == 0 and mt == 0 and kt == 0 else None,
                               pm_free[pj] if kt == 0 else None],
                        inc=(kt == KT - 1), mode=(128, 128, False))
                if pend is not None:
                    emit(pend[0], pend[1], tmm)
                pend = (mt, pj)
            tdr = p.op("tensor", lambda e: e.drain(), inc=True)
            emit(pend[0], pend[1], tdr)
            dep[j] = {"x": tdr}
            dep["pm"] = pm_free
            dep["o"] = o_free
        p.barrier()
        st.close()

    def ph_l1_mid(self):
        p = self.p
        st = contextlib.ExitStack()
        CW = self.sbuf("md_cw", (128, 64), F32, st)
        CB = self.sbuf("md_cb", (128, 16), F32, st)
        DTB = self.sbuf("md_dtb", (128, 16), F32, st)
        p.dma("sync", CW[:], self.dram["l1_convw"])
        p.dma("sync", CB[:], self.dram["l1_convb"])
        p.dma("sync", DTB[:], self.dram["l1_dtb"])
        WXD = self.sbuf("md_wxd", (128, 16 * 64), BF16, st)
        WXB = self.sbuf("md_wxb", (128, 16 * 16), BF16, st)
        WXC = self.sbuf("md_wxc", (128, 16 * 16), BF16, st)
        WDT = self.sbuf("md_wdt", (64, 2048), BF16, st)
        t1 = self.wload(WXD, "l1_xp_dt", 16 * 64)
        t2 = self.wload(WXB, "l1_xp_b", 16 * 16)
        t3 = self.wload(WXC, "l1_xp_c", 16 * 16)
        t4 = self.wload(WDT, "l1_dtw", 2048)
        p.barrier()
        UH = [self.sbuf(f"md_u{i}", (128, 16 * 516), F32, st) for i in range(2)]
        UCF = [self.sbuf(f"md_ucf{i}", (128, 16 * 512), F32, st) for i in range(2)]
        UCB = [self.sbuf(f"md_ucb{i}", (128, 16 * 512), BF16, st) for i in range(2)]
        DTR = [self.sbuf(f"md_dtr{i}", (64, 512), BF16, st) for i in range(2)]
        SR = [self.sbuf(f"md_sr{i}", (16, 512), F32, st) for i in range(4)]
        ARS = self.sbuf("md_ars", (128, 512 * 4), F32, st)
        DTO = [self.sbuf(f"md_dto{i}", (128, 512), F32, st) for i in range(2)]
        DUO = [self.sbuf(f"md_duo{i}", (128, 512), F32, st) for i in range(2)]
        DUB = [self.sbuf(f"md_dub{i}", (128, 512), BF16, st) for i in range(2)]
        PX = [self.psum(f"md_px{i}", (128, 512), F32, st) for i in range(4)]
        dep = {}
        for tc in range(NT // 512):
            j = tc % 2
            d = dep.get(j, {})
            b_, t0 = tc // 4, (tc % 4) * 512
            uview = self.dram["ut"].rearrange("p (k t) -> p k t", k=16)
            th0 = p.op("vector", lambda e, j=j: e.memset(
                UH[j][:].rearrange("p (k t) -> p k t", k=16)[:, :, 0:4], 0.0),
                waits=[d.get("uh")], inc=True)
            if t0 == 0:
                tlu = p.dma("sync", UH[j][:].rearrange("p (k t) -> p k t", k=16)[:, :, 4:516],
                            uview[:, :, tc * 512:(tc + 1) * 512], waits=[th0])
            else:
                tlu = p.dma("sync", UH[j][:].rearrange("p (k t) -> p k t", k=16)[:, :, 1:516],
                            uview[:, :, tc * 512 - 3:(tc + 1) * 512], waits=[th0])
            tconv = [None] * 16
            for dt_ in range(16):
                tacc = None
                for jj in range(4):
                    src = UH[j][:].rearrange("p (k t) -> p k t", k=16)[:, dt_, 1 + jj:513 + jj]
                    if jj == 0:
                        tacc = p.op("vector", lambda e, dt_=dt_, jj=jj, src=src: e.tensor_scalar(
                            ARS[:, 0:512], src, CW[:, dt_ * 4 + jj: dt_ * 4 + jj + 1], None, ALU.mult),
                            waits=[tlu, d.get("ars")], inc=True)
                    else:
                        tacc = p.op("vector", lambda e, dt_=dt_, jj=jj, src=src: e.scalar_tensor_tensor(
                            ARS[:, 0:512], src, CW[:, dt_ * 4 + jj: dt_ * 4 + jj + 1],
                            ARS[:, 0:512], ALU.mult, ALU.add),
                            waits=[tacc], inc=True)
                tf = p.op("scalar", lambda e, j=j, dt_=dt_: e.activation(
                    UCF[j][:, dt_ * 512:(dt_ + 1) * 512], ARS[:, 0:512], AF.Silu,
                    bias=CB[:, dt_:dt_ + 1]), waits=[tacc, d.get("ucf")], inc=True)
                tb = p.op("vector", lambda e, j=j, dt_=dt_: e.tensor_copy(
                    UCB[j][:, dt_ * 512:(dt_ + 1) * 512], UCF[j][:, dt_ * 512:(dt_ + 1) * 512]),
                    waits=[tf, d.get("ucb")], inc=True)
                tconv[dt_] = tb
            tconvl = tconv[-1]
            tsu = p.dma("sync", self.dram["ucf"].rearrange("p (k t) -> p k t", k=16)[
                :, :, tc * 512:(tc + 1) * 512], UCF[j][:].rearrange("p (k t) -> p k t", k=16),
                waits=[tconvl])
            tsb2 = p.dma("sync", self.dram["ucb"].rearrange("p (k t) -> p k t", k=16)[
                :, :, tc * 512:(tc + 1) * 512], UCB[j][:].rearrange("p (k t) -> p k t", k=16),
                waits=[tconvl])
            tslack = self.slack(tconvl)
            txd = None
            for kt in range(16):
                txd = p.op("tensor", lambda e, kt=kt, j=j: e.matmul(
                    PX[0][0:64, :], WXD[:, kt * 64:(kt + 1) * 64],
                    UCB[j][:, kt * 512:(kt + 1) * 512],
                    start=(kt == 0), stop=(kt == 15)),
                    waits=[tslack if kt == 0 else None, d.get("px0") if kt == 0 else None],
                    inc=(kt == 15), mode=(128, 64, False))
            txb = None
            for kt in range(16):
                txb = p.op("tensor", lambda e, kt=kt, j=j: e.matmul(
                    PX[1][0:16, :], WXB[:, kt * 16:(kt + 1) * 16],
                    UCB[j][:, kt * 512:(kt + 1) * 512],
                    start=(kt == 0), stop=(kt == 15)),
                    waits=[d.get("px1") if kt == 0 else None],
                    inc=(kt == 15), mode=(128, 32, False))
            txc = None
            for kt in range(16):
                txc = p.op("tensor", lambda e, kt=kt, j=j: e.matmul(
                    PX[2][0:16, :], WXC[:, kt * 16:(kt + 1) * 16],
                    UCB[j][:, kt * 512:(kt + 1) * 512],
                    start=(kt == 0), stop=(kt == 15)),
                    waits=[d.get("px2") if kt == 0 else None],
                    inc=(kt == 15), mode=(128, 32, False))
            tdrx = p.op("tensor", lambda e: e.drain(), inc=True)
            tevd = p.op("scalar", lambda e, j=j: e.copy(DTR[j][:], PX[0][0:64, :]),
                        waits=[tdrx, d.get("dtr")], inc=True)
            tevb = p.op("vector", lambda e: e.tensor_copy(SR[0][:], PX[1][0:16, :]),
                        waits=[tdrx, d.get("sr0")], inc=True)
            tevc = p.op("vector", lambda e: e.tensor_copy(SR[1][:], PX[2][0:16, :]),
                        waits=[tdrx, d.get("sr1")], inc=True)
            tstb = p.dma("sync", self.dram["bt_m"][:, tc * 512:(tc + 1) * 512], SR[0][:],
                         waits=[tevb])
            tstc = p.dma("sync", self.dram["ct_m"][:, tc * 512:(tc + 1) * 512], SR[1][:],
                         waits=[tevc])
            dep["sr0"] = tstb
            dep["sr1"] = tstc
            tevd_s = self.slack(tevd)
            pm_free = dep.get("pmdt", [None, None])
            dtf = dep.get("dtf", [None, None])
            duf = dep.get("duf", [None, None])
            dbf = dep.get("dbf", [None, None])
            arsf = dep.get("arsf", [None, None])
            dt_toks = []
            for mt in range(16):
                pj = mt % 2
                p.op("tensor", lambda e, mt=mt, pj=pj, j=j: e.matmul(
                    PX[pj][:], WDT[:, mt * 128:(mt + 1) * 128], DTR[j][:],
                    start=True, stop=True),
                    waits=[tevd_s if mt == 0 else None,
                           tevb if mt == 1 else None,
                           t4 if tc == 0 and mt == 0 else None,
                           pm_free[pj]], mode=(64, 128, False))
                tdr2 = p.op("tensor", lambda e: e.drain(), inc=True)
                a_ = ARS[:, pj * 1024:pj * 1024 + 512]
                ab = ARS[:, pj * 1024 + 512:pj * 1024 + 1024]
                ta = p.op("vector", lambda e, mt=mt, pj=pj, a_=a_: e.tensor_scalar(
                    a_, PX[pj][:], DTB[:, mt:mt + 1], None, ALU.add),
                    waits=[tdr2, arsf[pj]], inc=True)
                pm_free[pj] = ta
                tab = p.op("scalar", lambda e, a_=a_, ab=ab: e.activation(
                    ab, a_, AF.Abs), waits=[ta], inc=True)
                tex = p.op("scalar", lambda e, ab=ab: e.activation(ab, ab, AF.Exp, scale=-1.0),
                           waits=[tab], inc=True)
                tln = p.op("scalar", lambda e, ab=ab: e.activation(ab, ab, AF.Ln,
                           bias=self.c_onecol[:, 0:1]), waits=[tex], inc=True)
                trl = p.op("scalar", lambda e, a_=a_: e.activation(a_, a_, AF.Relu),
                           waits=[ta, tln] if False else [ta], inc=True)
                tdt = p.op("vector", lambda e, pj=pj, a_=a_, ab=ab: e.tensor_tensor(
                    DTO[pj][:], a_, ab, ALU.add), waits=[tln, trl, dtf[pj]], inc=True)
                arsf[pj] = tdt
                tdu = p.op("vector", lambda e, j=j, mt=mt, pj=pj: e.tensor_tensor(
                    DUO[pj][:], DTO[pj][:], UCF[j][:, mt * 512:(mt + 1) * 512], ALU.mult),
                    waits=[tdt, duf[pj]], inc=True)
                tdub = p.op("vector", lambda e, pj=pj: e.tensor_copy(DUB[pj][:], DUO[pj][:]),
                            waits=[tdu, dbf[pj]], inc=True)
                ts1 = p.dma("gpsimd", self.dram["dtt"].rearrange("p (k t) -> p k t", k=16)[
                    :, mt, tc * 512:(tc + 1) * 512], DTO[pj][:], waits=[tdt])
                ts3 = p.dma("gpsimd", self.dram["dub"].rearrange("p (k t) -> p k t", k=16)[
                    :, mt, tc * 512:(tc + 1) * 512], DUB[pj][:], waits=[tdub])
                dtf[pj] = ts1
                duf[pj] = tdub
                dbf[pj] = ts3
                dt_toks.append(tdu)
            dep["pmdt"] = pm_free
            dep["dtf"] = dtf
            dep["duf"] = duf
            dep["dbf"] = dbf
            dep["arsf"] = arsf
            dep[j] = {"uh": tsu, "ars": dt_toks[0], "ucf": dt_toks[-1], "ucb": tdrx,
                      "dtr": pm_free[0], "px0": tevd, "px1": tevb, "px2": tevc}
        p.barrier()
        st.close()

    def ph_l1_scan(self):
        p = self.p
        st = contextlib.ExitStack()
        AAR = self.sbuf("sc_a", (128, 256), F32, st)
        DCOL = self.sbuf("sc_d", (128, 16), F32, st)
        ta0 = p.dma("sync", AAR[:], self.dram["l1_alog"])
        p.dma("sync", DCOL[:], self.dram["l1_d"])
        tA = p.op("scalar", lambda e: e.activation(AAR[:], AAR[:], AF.Exp), waits=[ta0], inc=True)
        tA = p.op("vector", lambda e: e.tensor_scalar(AAR[:], AAR[:], -1.0, None, ALU.mult),
                  waits=[tA], inc=True)
        p.barrier()
        BRNB = self.sbuf("sc_brn", (128, 8 * T), BF16, st)
        CRNB = self.sbuf("sc_crn", (128, 8 * T), BF16, st)
        SROW = self.sbuf("sc_srow", (1, T), F32, st)
        HBN = self.sbuf("sc_hbn", (128, 8 * T), BF16, st)
        DTT = self.sbuf("sc_dt", (128, T), F32, st)
        DUB2 = self.sbuf("sc_du", (128, T), BF16, st)
        DAB = self.sbuf("sc_dab", (128, T), F32, st)
        DBUF = self.sbuf("sc_dbu", (128, T), BF16, st)
        YH = self.sbuf("sc_yh", (128, 8 * T), BF16, st)
        YACC = self.sbuf("sc_yacc", (128, T), F32, st)
        UCT = self.sbuf("sc_uct", (128, T), F32, st)
        ZBT = self.sbuf("sc_zb", (128, T), BF16, st)
        SZ = self.sbuf("sc_sz", (128, T), F32, st)
        YO = self.sbuf("sc_yo", (128, T), BF16, st)
        PR = self.psum("sc_pr", (128, 512), F32, st)
        dep = {}
        for b_ in range(BC):
            for nh in range(2):
                tprevB = dep.get("brn")
                for ni in range(8):
                    n = nh * 8 + ni
                    for (nm, dstb) in (("bt_m", BRNB), ("ct_m", CRNB)):
                        tr0 = p.dma("sync", SROW[:],
                                    self.dram[nm][n:n + 1, b_ * T:(b_ + 1) * T],
                                    waits=[tprevB])
                        for qc in range(T // 512):
                            p.op("tensor", lambda e, qc=qc: e.matmul(
                                PR[:], self.c_ones65[0:1, :], SROW[:, qc * 512:(qc + 1) * 512],
                                start=True, stop=True),
                                waits=[self.slack(tr0, 1) if qc == 0 else None, tprevB],
                                mode=(32, 128, False))
                            tdr = p.op("tensor", lambda e: e.drain(), inc=True)
                            tprevB = p.op("scalar", lambda e, ni=ni, qc=qc, dstb=dstb: e.copy(
                                dstb[:, ni * T + qc * 512: ni * T + (qc + 1) * 512], PR[:]),
                                waits=[tdr], inc=True)
                trepl = tprevB
                for dt_ in range(16):
                    d = dep
                    tld = p.dma("sync", DTT[:], self.dram["dtt"].rearrange("p (k t) -> p k t", k=16)[
                        :, dt_, b_ * T:(b_ + 1) * T], waits=[d.get("dtt")])
                    tlu = p.dma("sync", DUB2[:], self.dram["dub"].rearrange("p (k t) -> p k t", k=16)[
                        :, dt_, b_ * T:(b_ + 1) * T], waits=[d.get("dub")])
                    tscan = [None] * 8
                    for ni in range(8):
                        n = nh * 8 + ni
                        tda = p.op("scalar", lambda e, dt_=dt_, n=n: e.activation(
                            DAB[:], DTT[:], AF.Exp, scale=AAR[:, dt_ * 16 + n: dt_ * 16 + n + 1]),
                            waits=[tld, tA if (b_ == 0 and nh == 0 and dt_ == 0 and ni == 0) else None,
                                   d.get("dab")], inc=True)
                        tdb = p.op("vector", lambda e, ni=ni: e.tensor_tensor(
                            DBUF[:], DUB2[:], BRNB[:, ni * T:(ni + 1) * T], ALU.mult),
                            waits=[tlu, trepl if ni == 0 and dt_ == 0 else None,
                                   d.get("dbu")], inc=True)
                        tsc = p.op("vector", lambda e, ni=ni: e.tensor_tensor_scan(
                            HBN[:, ni * T:(ni + 1) * T], DAB[:], DBUF[:], 0.0,
                            ALU.mult, ALU.add),
                            waits=[tda, tdb, d.get(f"hbn{ni}")], inc=True)
                        tscan[ni] = tsc
                        dep["dab"] = tsc
                        dep["dbu"] = tsc
                    tym = p.op("vector", lambda e: e.tensor_tensor(
                        YH[:], HBN[:], CRNB[:], ALU.mult),
                        waits=tscan + [trepl if dt_ == 0 else None, d.get("yh")], inc=True)
                    v8 = lambda t, a, b2: t.rearrange("p (n t) -> p n t", n=8)[:, a:b2, :]
                    tt1 = p.op("vector", lambda e, v8=v8: e.tensor_tensor(
                        v8(YH[:], 0, 4), v8(YH[:], 0, 4), v8(YH[:], 4, 8), ALU.add),
                        waits=[tym], inc=True)
                    tt2 = p.op("vector", lambda e, v8=v8: e.tensor_tensor(
                        v8(YH[:], 0, 2), v8(YH[:], 0, 2), v8(YH[:], 2, 4), ALU.add),
                        waits=[tt1], inc=True)
                    tt3 = p.op("vector", lambda e, v8=v8: e.tensor_tensor(
                        v8(YH[:], 0, 1), v8(YH[:], 0, 1), v8(YH[:], 1, 2), ALU.add),
                        waits=[tt2], inc=True)
                    if nh == 0:
                        tya = p.dma("sync", self.dram["yacct"].rearrange("p (k t) -> p k t", k=16)[
                            :, dt_, :], YH[:, 0:T], waits=[tt3])
                    else:
                        tyl = p.dma("gpsimd", YACC[:], self.dram["yacct"].rearrange(
                            "p (k t) -> p k t", k=16)[:, dt_, :], waits=[d.get("yaccb")])
                        tya = p.op("vector", lambda e: e.tensor_tensor(
                            YACC[:], YACC[:], YH[:, 0:T], ALU.add),
                            waits=[tt3, tyl], inc=True)
                        tlc = p.dma("gpsimd", UCT[:], self.dram["ucf"].rearrange(
                            "p (k t) -> p k t", k=16)[:, dt_, b_ * T:(b_ + 1) * T],
                            waits=[d.get("uct")])
                        tlz = p.dma("gpsimd", ZBT[:], self.dram["zt"].rearrange(
                            "p (k t) -> p k t", k=16)[:, dt_, b_ * T:(b_ + 1) * T],
                            waits=[d.get("zbt")])
                        ty2 = p.op("vector", lambda e, dt_=dt_: e.scalar_tensor_tensor(
                            YACC[:], UCT[:], DCOL[:, dt_:dt_ + 1], YACC[:], ALU.mult, ALU.add),
                            waits=[tya, tlc], inc=True)
                        tsz = p.op("scalar", lambda e: e.activation(SZ[:], ZBT[:], AF.Silu),
                                   waits=[tlz, d.get("sz")], inc=True)
                        ty3 = p.op("vector", lambda e: e.tensor_tensor(YACC[:], YACC[:], SZ[:], ALU.mult),
                                   waits=[ty2, tsz], inc=True)
                        tyo = p.op("vector", lambda e: e.tensor_copy(YO[:], YACC[:]),
                                   waits=[ty3, d.get("yo")], inc=True)
                        tst = p.dma("sync", self.dram["yt"].rearrange("p (k t) -> p k t", k=16)[
                            :, dt_, b_ * T:(b_ + 1) * T], YO[:], waits=[tyo])
                        dep["yaccb"] = tyo
                        dep["uct"] = ty2
                        dep["zbt"] = tsz
                        dep["sz"] = ty3
                        dep["yo"] = tst
                    dep["yh"] = tya
                    for ni in range(8):
                        dep[f"hbn{ni}"] = tym
                    dep["dtt"] = tscan[-1]
                    dep["dub"] = tscan[-1]
                dep["brn"] = tym
        p.barrier()
        st.close()

    def ph_l1_out(self):
        p = self.p
        st = contextlib.ExitStack()
        WB = self.sbuf("lo_w", (128, 16 * 1024), BF16, st)
        tw = self.wload(WB, "l1_outproj", 16 * 1024)
        p.barrier()
        XCH = [self.sbuf(f"lo_x{i}", (128, 16 * 512), BF16, st) for i in range(2)]
        RES = [self.sbuf(f"lo_r{i}", (128, 512), F32, st) for i in range(2)]
        OB = [self.sbuf(f"lo_o{i}", (128, 512), F32, st) for i in range(2)]
        PM = [self.psum(f"lo_p{i}", (128, 512), F32, st) for i in range(2)]
        dep = {}
        for tc in range(NT // 512):
            j = tc % 2
            d = dep.get(j, {})
            tl0 = p.dma("sync", XCH[j][:],
                        self.dram["yt"].rearrange("p (k t) -> p k t", k=16)[:, :, tc * 512:(tc + 1) * 512],
                        waits=[d.get("x")])
            tl = self.slack(tl0)
            pm_free = dep.get("pm", [None, None])
            ob_free = dep.get("ob", [None, None])
            res_free = dep.get("rf", [None, None])

            def emit(mt, pj, safe, tc=tc):
                trl = p.dma("gpsimd", RES[pj][:], self.achunk("xnf", tc * 512, 512)[:, mt, :],
                            waits=[res_free[pj]])
                tev = p.op("vector", lambda e, pj=pj: e.tensor_tensor(
                    OB[pj][:], PM[pj][:], RES[pj][:], ALU.add),
                    waits=[safe, trl, ob_free[pj]], inc=True)
                pm_free[pj] = tev
                res_free[pj] = tev
                tst = p.dma("scalar", self.achunk("smix", tc * 512, 512)[:, mt, :],
                            OB[pj][:], waits=[tev])
                ob_free[pj] = tst

            pend = None
            for mt in range(8):
                pj = mt % 2
                tmm = None
                for kt in range(16):
                    tmm = p.op("tensor", lambda e, mt=mt, kt=kt, pj=pj, j=j: e.matmul(
                        PM[pj][:], WB[:, kt * 1024 + mt * 128: kt * 1024 + (mt + 1) * 128],
                        XCH[j][:, kt * 512:(kt + 1) * 512],
                        start=(kt == 0), stop=(kt == 15)),
                        waits=[tl if mt == 0 and kt == 0 else None,
                               tw if tc == 0 and mt == 0 and kt == 0 else None,
                               pm_free[pj] if kt == 0 else None],
                        inc=(kt == 15), mode=(128, 128, False))
                if pend is not None:
                    emit(pend[0], pend[1], tmm)
                pend = (mt, pj)
            tdr = p.op("tensor", lambda e: e.drain(), inc=True)
            emit(pend[0], pend[1], tdr)
            dep[j] = {"x": tmm}
            dep["pm"] = pm_free
            dep["ob"] = ob_free
            dep["rf"] = res_free
        p.barrier()
        st.close()

    def ph_l1f_front(self):
        p = self.p
        st = contextlib.ExitStack()
        W = 512
        NCH = NT // W
        Gam = self.sbuf("lf_g", (128, 8), F32, st)
        Bet = self.sbuf("lf_b", (128, 8), F32, st)
        p.dma("sync", Gam[:], self.dram["l1f_lnsc"])
        p.dma("sync", Bet[:], self.dram["l1f_lnbi"])
        XCH = [self.sbuf(f"lf_x{i}", (128, KT * W), F32, st) for i in range(2)]
        XB = [self.sbuf(f"lf_xb{i}", (128, KT * W), BF16, st) for i in range(2)]
        XSQ = [self.sbuf(f"lf_xsq{i}", (128, KT * W), BF16, st) for i in range(2)]
        ROWS = [self.sbuf(f"lf_rows{i}", (1, W * 5), F32, st) for i in range(2)]
        RSB = [self.sbuf(f"lf_rsb{i}", (128, W), F32, st) for i in range(2)]
        MRSB = [self.sbuf(f"lf_mrsb{i}", (128, W), F32, st) for i in range(2)]
        XN = [self.sbuf(f"lf_xn{i}", (128, KT * W), F32, st) for i in range(2)]
        XNB = [self.sbuf(f"lf_xnb{i}", (128, KT * W), BF16, st) for i in range(2)]
        PS = [self.psum(f"lf_ps{i}", (1, W), F32, st) for i in range(2)]
        PQ = [self.psum(f"lf_pq{i}", (1, W), F32, st) for i in range(2)]
        PR1 = [self.psum(f"lf_pr1{i}", (128, W), F32, st) for i in range(2)]
        PR2 = [self.psum(f"lf_pr2{i}", (128, W), F32, st) for i in range(2)]
        p.barrier()
        dep = {}
        for c in range(NCH):
            j = c % 2
            d = dep.get(j, {})
            tl = p.dma("sync", XCH[j][:], self.achunk("xnf", c * W, W), waits=[d.get("xch")])
            txb = p.op("vector", lambda e, j=j: e.tensor_copy(XB[j][:], XCH[j][:]),
                       waits=[tl, d.get("xb")], inc=True)
            txb = self.slack(txb)
            tsq = p.op("scalar", lambda e, j=j: e.activation(XSQ[j][:], XCH[j][:], AF.Square),
                       waits=[tl, d.get("xsq")], inc=True)
            tsq = self.slack(tsq, 1)
            for kt in range(KT):
                p.op("tensor", lambda e, j=j, kt=kt: e.matmul(
                    PS[j][:], self.c_ones[:], XB[j][:, kt * W:(kt + 1) * W],
                    start=(kt == 0), stop=(kt == KT - 1)),
                    waits=[txb, d.get("ps")] if kt == 0 else (), mode=(128, 32, False))
            for kt in range(KT):
                p.op("tensor", lambda e, j=j, kt=kt: e.matmul(
                    PQ[j][:], self.c_ones[:], XSQ[j][:, kt * W:(kt + 1) * W],
                    start=(kt == 0), stop=(kt == KT - 1)),
                    waits=[tsq, d.get("pq")] if kt == 0 else (), mode=(128, 32, False))
            tstat = p.op("tensor", lambda e: e.drain(), inc=True)
            R = ROWS[j]
            mrow, qrow, vrow = R[:, 0:W], R[:, W:2 * W], R[:, 2 * W:3 * W]
            rrow, mrrow = R[:, 3 * W:4 * W], R[:, 4 * W:5 * W]
            tmrow = p.op("vector", lambda e, mrow=mrow, j=j: e.tensor_scalar(
                mrow, PS[j][:], 1.0 / D, None, ALU.mult), waits=[tstat, d.get("rows")], inc=True)
            tps = p.op("vector", lambda e, qrow=qrow, j=j: e.tensor_scalar(
                qrow, PQ[j][:], 1.0 / D, None, ALU.mult), inc=True)
            tv1_ = p.op("vector", lambda e, vrow=vrow, mrow=mrow: e.tensor_tensor(
                vrow, mrow, mrow, ALU.mult), waits=[tps], inc=True)
            tvar = p.op("vector", lambda e, vrow=vrow, qrow=qrow: e.tensor_tensor(
                vrow, qrow, vrow, ALU.subtract), waits=[tv1_], inc=True)
            p.op("scalar", lambda e, vrow=vrow: e.activation(vrow, vrow, AF.Ln,
                 bias=self.c_misc[0:1, 0:1]), waits=[tvar])
            trs = p.op("scalar", lambda e, rrow=rrow, vrow=vrow: e.activation(
                rrow, vrow, AF.Exp, scale=-0.5), inc=True)
            tmr = p.op("vector", lambda e, mrrow=mrrow, mrow=mrow, rrow=rrow: e.tensor_tensor(
                mrrow, mrow, rrow, ALU.mult), waits=[trs, tmrow], inc=True)
            tmr = self.slack(tmr)
            p.op("tensor", lambda e, j=j, rrow=rrow: e.matmul(
                PR1[j][:], self.c_ones65[0:1, :], rrow, start=True, stop=True),
                waits=[tmr, d.get("pr")], mode=(32, 128, False))
            p.op("tensor", lambda e, j=j, mrrow=mrrow: e.matmul(
                PR2[j][:], self.c_ones65[0:1, :], mrrow, start=True, stop=True),
                mode=(32, 128, False))
            trep = p.op("tensor", lambda e: e.drain(), inc=True)
            p.op("scalar", lambda e, j=j: e.copy(RSB[j][:], PR1[j][:]),
                 waits=[trep, d.get("rsb")])
            tevr = p.op("scalar", lambda e, j=j: e.copy(MRSB[j][:], PR2[j][:]), inc=True)
            bview = lambda t: t[:].unsqueeze(1).broadcast_to([128, KT, W])
            xview = lambda t: t[:].rearrange("p (k t) -> p k t", k=KT)
            tmul = p.op("vector", lambda e, j=j, bview=bview, xview=xview: e.tensor_tensor(
                xview(XN[j]), xview(XCH[j]), bview(RSB[j]), ALU.mult),
                waits=[tevr, d.get("xn")], inc=True)
            tsub = p.op("vector", lambda e, j=j, bview=bview, xview=xview: e.tensor_tensor(
                xview(XN[j]), xview(XN[j]), bview(MRSB[j]), ALU.subtract),
                waits=[tmul], inc=True)
            tgb = None
            for kt in range(KT):
                tgb = p.op("vector", lambda e, j=j, kt=kt: e.tensor_scalar(
                    XN[j][:, kt * W:(kt + 1) * W], XN[j][:, kt * W:(kt + 1) * W],
                    Gam[:, kt:kt + 1], Bet[:, kt:kt + 1], ALU.mult, ALU.add),
                    waits=[tsub] if kt else [tsub], inc=True)
            tgb = p.op("vector", lambda e, j=j: e.tensor_copy(XNB[j][:], XN[j][:]),
                       waits=[tgb, d.get("xnb")], inc=True)
            tsb = p.dma("sync", self.achunk("xn2t", c * W, W), XNB[j][:], waits=[tgb])
            dep[j] = {"xch": tmul, "xb": tstat, "xsq": tstat, "ps": tmrow, "pq": tps,
                      "rows": trep, "pr": tevr, "rsb": tgb, "xn": tgb, "xnb": tsb}
        p.barrier()
        st.close()
        st2 = contextlib.ExitStack()
        WBa = self.sbuf("lfa_w", (128, KT * 1024), BF16, st2)
        twa = self.wload(WBa, "l1f_attnw", KT * 1024)
        ABa = self.sbuf("lfa_ab", (128, 8), F32, st2)
        p.dma("sync", ABa[:], self.dram["l1f_attnb"])
        p.barrier()
        XCHa = [self.sbuf(f"lfa_x{i}", (128, KT * 512), BF16, st2) for i in range(2)]
        RESa = [self.sbuf(f"lfa_r{i}", (128, 512), F32, st2) for i in range(2)]
        TTa = [self.sbuf(f"lfa_t{i}", (128, 512), F32, st2) for i in range(2)]
        OBa = [self.sbuf(f"lfa_o{i}", (128, 512), BF16, st2) for i in range(2)]
        PMa = [self.psum(f"lfa_p{i}", (128, 512), F32, st2) for i in range(2)]
        depa = {}
        for tc in range(NT // 512):
            j = tc % 2
            da = depa.get(j, {})
            tla0a = p.dma("sync", XCHa[j][:], self.achunk("xn2t", tc * 512, 512), waits=[da.get("x")])
            tla = self.slack(tla0a)
            pm_freea = depa.get("pm", [None, None])
            ob_freea = depa.get("ob", [None, None])
            rfa = depa.get("rfa", [None, None])

            def emita(mt, pj, safe, tc=tc):
                trl = p.dma("gpsimd", RESa[pj][:], self.achunk("xnf", tc * 512, 512)[:, mt, :],
                            waits=[rfa[pj]])
                tt1 = p.op("vector", lambda e, mt=mt, pj=pj: e.tensor_scalar(
                    TTa[pj][:], PMa[pj][:], ABa[:, mt:mt + 1], None, ALU.add),
                    waits=[safe], inc=True)
                pm_freea[pj] = tt1
                tt2 = p.op("vector", lambda e, pj=pj: e.tensor_tensor(
                    OBa[pj][:], TTa[pj][:], RESa[pj][:], ALU.add),
                    waits=[tt1, trl, ob_freea[pj]], inc=True)
                rfa[pj] = tt2
                tst = p.dma("scalar", self.achunk("xat", tc * 512, 512)[:, mt, :],
                            OBa[pj][:], waits=[tt2])
                ob_freea[pj] = tst

            penda = None
            for mt in range(8):
                pj = mt % 2
                tmma = None
                for kt in range(KT):
                    tmma = p.op("tensor", lambda e, mt=mt, kt=kt, pj=pj, j=j: e.matmul(
                        PMa[pj][:], WBa[:, kt * 1024 + mt * 128: kt * 1024 + (mt + 1) * 128],
                        XCHa[j][:, kt * 512:(kt + 1) * 512],
                        start=(kt == 0), stop=(kt == KT - 1)),
                        waits=[tla if mt == 0 and kt == 0 else None,
                               twa if tc == 0 and mt == 0 and kt == 0 else None,
                               pm_freea[pj] if kt == 0 else None],
                        inc=(kt == KT - 1), mode=(128, 128, False))
                if penda is not None:
                    emita(penda[0], penda[1], tmma)
                penda = (mt, pj)
            tdra = p.op("tensor", lambda e: e.drain(), inc=True)
            emita(penda[0], penda[1], tdra)
            depa[j] = {"x": tmma}
            depa["pm"] = pm_freea
            depa["ob"] = ob_freea
            depa["rfa"] = rfa
        p.barrier()
        st2.close()

    def build(self):
        self.declare()
        self.load_consts()
        res_in = "x_t"
        final = {self.layers[-1]: True}
        for L in self.layers:
            dst = "out_t" if L == self.layers[-1] else ("res1" if res_in != "res1" else "res2")
            if L == 0:
                self.ph_ln(res_in, want_f32=False)
                self.ph_l0_qkv()
                self.ph_attn(0)
                self.ph_mlp_4096("l0_w1", "l0_b1", "l0_w2", "l0_b2", AF.Gelu, AF.Silu)
                self.ph_combine(res_in, dst)
            elif L == 1:
                self.ph_ln(res_in, want_f32=True)
                self.ph_l1_rms()
                self.ph_l1_inproj()
                self.ph_l1_mid()
                self.ph_l1_scan()
                self.ph_l1_out()
                self.ph_l1f_front()
                self.ph_mlp_4096("l1f_w1_0", "l1f_b1_0", "l1f_w2_0", "l1f_b2_0",
                                 AF.Gelu, None, src="xat", dst="eot", tcs=range(0, 4))
                self.ph_mlp_4096("l1f_w1_1", "l1f_b1_1", "l1f_w2_1", "l1f_b2_1",
                                 AF.Gelu, None, src="xat", dst="eot", tcs=range(4, 8))
                self.ph_gemm("eot", "gffn", "l1f_nextw", 8, 8, AF.Gelu,
                             biasname="l1f_nextb", act2=AF.Silu)
                self.ph_combine(res_in, dst)
            elif L == 2:
                self.ph_ln(res_in, want_f32=False)
                self.ph_qkv_fm(2)
                self.ph_attn(2)
                self.ph_l2_gate()
                self.ph_l2_ffn()
                self.ph_combine(res_in, dst)
            elif L == 3:
                self.ph_ln(res_in, want_f32=False)
                self.ph_qkv_fm(3)
                self.ph_attn(3)
                self.ph_l3_gate()
                self.ph_gemm("hbig1", "hbig2", "l3_h1", 32, 32, AF.Gelu, scalename="l3_sh1", mt0=0, mt1=16)
                self.ph_gemm("hbig1", "hbig2", "l3_h1", 32, 32, AF.Gelu, scalename="l3_sh1", mt0=16, mt1=32)
                self.ph_gemm("hbig2", "hbig1", "l3_h2", 32, 32, AF.Gelu, scalename="l3_sh2", mt0=0, mt1=16)
                self.ph_gemm("hbig2", "hbig1", "l3_h2", 32, 32, AF.Gelu, scalename="l3_sh2", mt0=16, mt1=32)
                self.ph_gemm("hbig1", "gffn", "l3_w3", 32, 8, AF.Silu, scalename="l3_s3")
                self.ph_combine(res_in, dst)
            res_in = dst
        for t in self.taps:
            ap = self.dram[t]
            sh = list(ap.shape)
            dt_ = ap.dtype
            out = self.dout("tap_" + t, sh, dt_)
            self.p.dma("sync", out, ap)
        self.p.barrier()
        self.p.op("sync", lambda e: e.nop())
        self.p.build()
        return self.nc


def build_nc(layers=(0,), taps=()):
    b = Builder(layers=layers, taps=taps)
    b.build()
    return b


def kernel(**inputs):
    per_core = host_prepare(inputs)
    b = build_nc(layers=(0, 1, 2, 3))
    res = run_bass_kernel_spmd(b.nc, per_core, core_ids=list(range(NCORES)))
    outs = []
    for c in range(NCORES):
        o = np.asarray(res.results[c]["out_t"], dtype=np.float32)
        xT = o.reshape(128, KT, NT).transpose(1, 0, 2).reshape(D, NT)
        outs.append(xT.T.reshape(BC, T, D))
    return np.concatenate(outs, axis=0)
